# version 07
# baseline (speedup 1.0000x reference)
import numpy as np
import concourse.bass as bass
import concourse.mybir as mybir
from concourse.bass_utils import run_bass_kernel_spmd

F32 = mybir.dt.float32
BF16 = mybir.dt.bfloat16
I32 = mybir.dt.int32
AF = mybir.ActivationFunctionType
ALU = mybir.AluOpType
AX = mybir.AxisListType

D = 4096
DC = D // 128
TOK = 1024
S = 2048
HID = 11008
HC = HID // 128
NH = 32
ISCALE = 1.0 / float(np.sqrt(128.0))

ENGINES = ("sync", "scalar", "vector", "tensor", "gpsimd")
LIMITS = {}
SBUF_LIMIT = 212800
_DT_SIZE = {F32: 4, BF16: 2, I32: 4}


class Builder:
    def __init__(self, dbg=False, upto=9, race_detect=False):
        self.dbg = dbg
        self.upto = upto
        self.nc = bass.Bass("TRN2", target_bir_lowering=False, debug=False,
                            detect_race_conditions=race_detect)
        self.ops = {e: [] for e in ENGINES}
        self.sems = {}
        self.cnt = {}
        for s in ("pe", "va", "sa", "ga"):
            self.sem(s)
        self._rr = {}
        for pool, n in (("ld", 8), ("st", 8), ("so", 4)):
            self._rr[pool] = 0
            for k in range(n):
                self.sem(f"{pool}{k}")
        self._pool_n = {"ld": 8, "st": 8, "so": 4}
        self._declare_dram()
        psA = self.nc.alloc_psum_tensor("psA", [128, 2048], F32)
        psB = self.nc.alloc_psum_tensor("psB", [128, 2048], F32)
        self.banks = [psA[:, 0:512], psA[:, 512:1024], psA[:, 1024:1536],
                      psA[:, 1536:2048], psB[:, 0:512], psB[:, 512:1024],
                      psB[:, 1024:1536], psB[:, 1536:2048]]
        self.psA = psA
        self.psB = psB
        self.bank_last = {i: ("va", 0) for i in range(8)}
        self._sbuf_off = 0
        self._persistent_sbuf()

    def sem(self, name):
        if name not in self.sems:
            self.sems[name] = self.nc.alloc_semaphore(name)
            self.cnt[name] = 0
        return name

    def emit(self, eng, fn, waits=(), inc=None, inc_n=1):
        wl = []
        seen = {}
        for s, v in waits:
            assert v <= self.cnt[s], f"wait on future inc: {s} {v} > {self.cnt[s]}"
            if v > 0 and seen.get(s, -1) < v:
                seen[s] = v
        wl = [(self.sems[s], v) for s, v in seen.items()]
        if inc is not None:
            self.cnt[inc] += inc_n
        sem_h = self.sems[inc] if inc is not None else None

        def run(e, fn=fn, wl=wl, sem_h=sem_h, inc_n=inc_n, eng=eng):
            for h, v in wl:
                e.wait_ge(h, v)
            inst = fn(e)
            if sem_h is not None:
                if inst is None:
                    if eng == "tensor":
                        e.drain()
                    inst = e.nop()
                inst.then_inc(sem_h, inc_n)
        self.ops[eng].append(run)
        return self.cnt[inc] if inc is not None else None

    def v(self, sem):
        return self.cnt[sem]

    def _dma(self, eng, pool, out, in_, waits):
        k = self._rr[pool] % self._pool_n[pool]
        self._rr[pool] += 1
        sem = f"{pool}{k}"
        w = list(waits) + [(sem, self.cnt[sem])]
        val = self.emit(eng, lambda e, o=out, i=in_: e.dma_start(out=o, in_=i),
                        waits=w, inc=sem, inc_n=16)
        return (sem, val)

    def load(self, out, in_, waits=()):
        return self._dma("sync", "ld", out, in_, waits)

    def store(self, out, in_, waits=(), pool="st"):
        return self._dma("scalar", pool, out, in_, waits)

    def all_dma(self, pool):
        return [(f"{pool}{k}", self.cnt[f"{pool}{k}"])
                for k in range(self._pool_n[pool])]

    def sb(self, name, shape, dtype, offset):
        per_part = int(np.prod(shape[1:])) * _DT_SIZE[dtype]
        assert offset + per_part <= SBUF_LIMIT, f"SBUF overflow: {name} @{offset}+{per_part}"
        origin = 16512
        assert origin >= self.nc.sbuf_base
        assert origin + offset + per_part <= self.nc.sbuf_top
        return self.nc.alloc_sbuf_tensor_at(name, shape, dtype, offset=origin + offset)

    def finalize(self):
        nc = self.nc
        with nc.Block() as block:
            for e in ENGINES:
                lst = self.ops[e]
                if lst:
                    getattr(block, e)(lambda eng, lst=lst: [f(eng) for f in lst])
        for s, c in self.cnt.items():
            assert c < 60000, f"sem {s} count {c} too high"
        return nc

    def bank_waits(self, bks):
        return [self.bank_last[b] for b in bks]

    def set_reader(self, bks, sem, val):
        for b in bks:
            old = self.bank_last[b]
            if old[0] == sem:
                self.bank_last[b] = (sem, max(old[1], val))
            else:
                self.bank_last[b] = (sem, val)

    def tp4(self, bank, srcs, outh, outl, tr_waits=(), evh_waits=(), evl_waits=()):
        n = len(srcs)
        pb = self.banks[bank]
        w = list(tr_waits) + self.bank_waits([bank]) + [("va", self.ident_ready)]

        def tr(e, srcs=srcs, pb=pb):
            inst = None
            for j, s in enumerate(srcs):
                inst = e.transpose(pb[:, 128 * j:128 * j + 128], s, self.ident_f[:])
            return inst
        pdone = self.emit("tensor", tr, waits=w, inc="pe")
        if len(outh.shape) == 2:
            outh = outh.rearrange("p (a b) -> p a b", b=128)
        if len(outl.shape) == 2:
            outl = outl.rearrange("p (a b) -> p a b", b=128)
        pv = pb[:, 0:n * 128].rearrange("p (a b) -> p a b", b=128)
        evh = self.emit("scalar",
                        lambda e, outh=outh, pv=pv: e.activation(out=outh, in_=pv,
                                                                 func=AF.Copy),
                        waits=[("pe", pdone)] + list(evh_waits), inc="sa")
        evl = self.emit("vector",
                        lambda e, outl=outl, outh=outh, pv=pv: e.tensor_tensor(
                            out=outl, in0=pv, in1=outh, op=ALU.subtract),
                        waits=[("sa", evh)] + list(evl_waits), inc="va")
        self.set_reader([bank], "va", evl)
        return pdone, evh, evl

    def _declare_dram(self):
        nc = self.nc
        dbgkind = "ExternalOutput" if self.dbg else "Internal"
        T = lambda n, s, d, k: nc.dram_tensor(n, s, d, kind=k).ap()
        self.x_own = T("x_own", [TOK, D], F32, "ExternalInput")
        self.x_sib = T("x_sib", [TOK, D], F32, "ExternalInput")
        self.wq = T("wq", [D, D], F32, "ExternalInput")
        self.wk = T("wk", [D, D], F32, "ExternalInput")
        self.wv = T("wv", [D, D], F32, "ExternalInput")
        self.wo = T("wo", [D, D], F32, "ExternalInput")
        self.w1 = T("w1", [352256, 64], I32, "ExternalInput")
        self.w2 = T("w2", [352256, 64], I32, "ExternalInput")
        self.w3 = T("w3", [352256, 64], I32, "ExternalInput")
        self.s1 = T("s1", [704512], F32, "ExternalInput")
        self.s2 = T("s2", [704512], F32, "ExternalInput")
        self.s3 = T("s3", [704512], F32, "ExternalInput")
        self.out = T("out", [TOK, D], F32, "ExternalOutput")
        self.w2t_h = T("w2t_h", [HC, 128, D], BF16, dbgkind)
        self.w2t_l = T("w2t_l", [HC, 128, D], BF16, "Internal")
        self.kT_h = T("kT_h", [D, S], BF16, dbgkind)
        self.kT_l = T("kT_l", [D, S], BF16, "Internal")
        self.qT_h = T("qT_h", [D, TOK], BF16, dbgkind)
        self.qT_l = T("qT_l", [D, TOK], BF16, "Internal")
        self.v_f = T("v_f", [D, S], F32, dbgkind)
        self.attnT_h = T("attnT_h", [8, 128, D], BF16, dbgkind)
        self.attnT_l = T("attnT_l", [8, 128, D], BF16, "Internal")
        self.x1 = T("x1", [TOK, D], F32, dbgkind)
        self.g_h = T("g_h", [HC, 128, TOK], BF16, dbgkind)
        self.g_l = T("g_l", [HC, 128, TOK], BF16, "Internal")

    def _persistent_sbuf(self):
        off = 0

        def nxt(name, shape, dtype):
            nonlocal off
            t = self.sb(name, shape, dtype, off)
            per_part = int(np.prod(shape[1:])) * _DT_SIZE[dtype]
            off += ((per_part + 31) // 32) * 32
            return t
        self.ident_f = nxt("ident_f", [128, 128], F32)
        self.eps1 = nxt("eps1", [128, 1], F32)
        self.sx = [nxt(f"sx{i}", [128, 1], F32) for i in range(2)]
        self.mean = [nxt(f"mean{i}", [128, 1], F32) for i in range(2)]
        self.sxc2 = [nxt(f"sxc2{i}", [128, 1], F32) for i in range(2)]
        self.sdv = [nxt(f"sdv{i}", [128, 1], F32) for i in range(2)]
        self.inv = [nxt(f"inv{i}", [128, 1], F32) for i in range(2)]
        self.mrow = [nxt(f"mrow{i}", [128, 1], F32) for i in range(2)]
        self.negm = [nxt(f"negm{i}", [128, 1], F32) for i in range(2)]
        self.zrow = self.sdv
        self.rz = [nxt(f"rz{i}", [128, 1], F32) for i in range(2)]
        self.parts = nxt("parts", [128, 64], F32)
        self.base = (off + 63) // 64 * 64
        self.ident_i = self.sb("ident_i", [128, 128], I32, self.base)

    def p_init(self):
        self.emit("gpsimd",
                  lambda e: e.iota(self.ident_i[:, :], [[1, 128]], base=0,
                                   channel_multiplier=-1),
                  inc="ga")
        self.ident_ready = None
        iv = self.emit(
            "vector",
            lambda e: e.tensor_scalar(out=self.ident_f[:], in0=self.ident_i[:],
                                      scalar1=0.0, scalar2=None, op0=ALU.is_equal),
            waits=[("ga", self.v("ga"))], inc="va")
        self.emit("vector", lambda e: e.memset(self.eps1[:], 1e-5), inc="va")
        self.ident_ready = self.v("va")
        self.emit("tensor",
                  lambda e: e.matmul(self.banks[7][:, 0:128], self.ident_f[:],
                                     self.ident_f[:], start=True, stop=True),
                  waits=[("va", self.ident_ready)], inc="pe")

    def _prep_one(self, pkv, scv, dsth, dstl, nrows, ngroups, pk, scb, msb, lsb, wn,
                  stg_h, stg_l, state, stg_h2=None, stg_l2=None):
        nw = ngroups * 64
        wnf = wn.rearrange("p a b -> p (a b)")
        for rc in range(nrows):
            gr = state.get("row", 0)
            state["row"] = gr + 1
            sl = gr % 2
            w = [("va", state["ext"].get(gr - 2, 0))]
            t0 = self.load(pk[sl][:, 0:nw], pkv[rc * 128:(rc + 1) * 128, :], waits=w)
            t1 = self.load(scb[sl][:, 0:2 * ngroups],
                           scv[rc * 128:(rc + 1) * 128, :], waits=w)

            def ext(e, sl=sl, ng=ngroups):
                e.tensor_scalar(out=msb[:, 0:nw], in0=pk[sl][:, 0:nw], scalar1=4,
                                scalar2=None, op0=ALU.arith_shift_right)
                e.tensor_scalar(out=lsb[:, 0:nw], in0=pk[sl][:, 0:nw], scalar1=28,
                                scalar2=28, op0=ALU.arith_shift_left,
                                op1=ALU.arith_shift_right)
                s_m = scb[sl][:, 0:2 * ng:2].unsqueeze(-1).broadcast_to([128, ng, 64])
                s_l = scb[sl][:, 1:2 * ng:2].unsqueeze(-1).broadcast_to([128, ng, 64])
                e.tensor_tensor(out=wn[:, :, 0:64],
                                in0=msb[:, 0:nw].rearrange("p (i c) -> p i c", c=64),
                                in1=s_m, op=ALU.mult)
                return e.tensor_tensor(
                    out=wn[:, :, 64:128],
                    in0=lsb[:, 0:nw].rearrange("p (i c) -> p i c", c=64),
                    in1=s_l, op=ALU.mult)
            exw = [t0, t1, ("pe", state["tp"].get(gr - 1, 0))]
            state["ext"][gr] = self.emit("vector", ext, waits=exw, inc="va")

            if stg_h2 is not None and sl == 1:
                cur_h, cur_l, skey = stg_h2, stg_l2, 1
            else:
                cur_h, cur_l, skey = stg_h, stg_l, 0
            last_pd = 0
            ntp = (ngroups + 3) // 4
            for g in range(ntp):
                n = min(4, ngroups - 4 * g)
                bank = g % 4
                srcs = [wnf[:, 128 * (4 * g + j):128 * (4 * g + j) + 128]
                        for j in range(n)]
                osl = slice(g * 512, g * 512 + n * 128)
                stoks = state["stored"].get(skey, [])
                pd, eh, el = self.tp4(
                    bank, srcs, cur_h[:, osl], cur_l[:, osl],
                    tr_waits=[("va", state["ext"][gr])],
                    evh_waits=stoks, evl_waits=stoks)
                last_pd = pd
                last_eh, last_el = eh, el
            state["tp"][gr] = last_pd
            oh = dsth(rc)
            ol = dstl(rc)
            ih = cur_h[:, 0:ngroups * 128]
            il = cur_l[:, 0:ngroups * 128]
            if len(oh.shape) == 3:
                ih = ih.rearrange("p (a b) -> p a b", b=128)
                il = il.rearrange("p (a b) -> p a b", b=128)
            s1_ = self.store(oh, ih, waits=[("sa", last_eh)])
            s2_ = self.store(ol, il, waits=[("va", last_el)])
            state["stored"][skey] = [s1_, s2_]

    def p_prep(self):
        base = self.base
        pk = [self.sb(f"pk{i}", [128, 5504], I32, base + i * 22016) for i in range(2)]
        scb = [self.sb(f"scb{i}", [128, 172], F32, base + 44032 + i * 704)
               for i in range(2)]
        msb = self.sb("msb_p", [128, 5504], I32, base + 45440)
        lsb = self.sb("lsb_p", [128, 5504], I32, base + 67456)
        wn = self.sb("wn_p", [128, HC, 128], F32, base + 89472)
        stg_h = self.sb("pstg_h", [128, HC * 128], BF16, base + 133504)
        stg_l = self.sb("pstg_l", [128, HC * 128], BF16, base + 155520)

        st13 = {"ext": {}, "tp": {}, "stored": {}}

        pkv2 = self.w2.rearrange("(o i) c -> o (i c)", i=HC)
        scv2 = self.s2.rearrange("(o g) -> o g", g=172)

        def d2h(oc):
            return self.w2t_h[0:HC, :, oc * 128:(oc + 1) * 128].transpose([1, 0, 2])

        def d2l(oc):
            return self.w2t_l[0:HC, :, oc * 128:(oc + 1) * 128].transpose([1, 0, 2])
        if not LIMITS.get("skip_w2"):
            self._prep_one(pkv2, scv2, d2h, d2l, LIMITS.get("w2_rows", DC), HC,
                           pk, scb, msb, lsb, wn.ap(), stg_h[:], stg_l[:], st13)
        self.prep_store_done = self.all_dma("st")

    def ln_chunk(self, xsrc, xa, xb, dsth, dstl, pre_waits=(), sum_from_parts=None):
        last_ev = {}
        xln_done = {}
        for tt in range(8):
            sl = tt % 2
            lw = list(pre_waits) + [("va", xln_done.get(tt - 2, 0)),
                                    ("va", last_ev.get((tt - 2, 7), 0))]
            ldt = self.load(xa[sl][:], xsrc[tt], waits=lw)

            if sum_from_parts is None:
                def p1(e, sl=sl):
                    e.tensor_scalar(out=xb[sl][:], in0=xa[sl][:], scalar1=0.0,
                                    scalar2=None, op0=ALU.add, op1=ALU.add,
                                    accum_out=self.sx[sl][:])
                    e.drain()
                    e.tensor_scalar(out=self.mean[sl][:], in0=self.sx[sl][:],
                                    scalar1=1.0 / D, scalar2=None, op0=ALU.mult)
                    e.drain()
                    return e.tensor_scalar(out=xb[sl][:], in0=xa[sl][:],
                                           scalar1=self.mean[sl][:], scalar2=None,
                                           op0=ALU.subtract)
            else:
                psl = sum_from_parts(tt)

                def p1(e, sl=sl, psl=psl):
                    e.reduce_sum(out=self.sx[sl][:], in_=psl, axis=AX.X)
                    e.drain()
                    e.tensor_scalar(out=self.mean[sl][:], in0=self.sx[sl][:],
                                    scalar1=1.0 / D, scalar2=None, op0=ALU.mult)
                    e.drain()
                    return e.tensor_scalar(out=xb[sl][:], in0=xa[sl][:],
                                           scalar1=self.mean[sl][:], scalar2=None,
                                           op0=ALU.subtract)
            xc_done = self.emit("vector", p1, waits=[ldt], inc="va")

            def p2(e, sl=sl):
                e.activation(out=xa[sl][:], in_=xb[sl][:], func=AF.Square,
                             accum_out=self.sxc2[sl][:])
                e.drain()
                return e.activation(out=self.sdv[sl][:], in_=self.sxc2[sl][:],
                                    func=AF.Sqrt, scale=1.0 / D, bias=self.eps1[:])
            sq_done = self.emit("scalar", p2, waits=[("va", xc_done)], inc="sa")

            def p3(e, sl=sl):
                e.reciprocal(out=self.inv[sl][:], in_=self.sdv[sl][:])
                e.drain()
                return e.tensor_scalar(out=xa[sl][:], in0=xb[sl][:],
                                       scalar1=self.inv[sl][:], scalar2=None,
                                       op0=ALU.mult)
            xln_done[tt] = self.emit("vector", p3, waits=[("sa", sq_done)], inc="va")

            for g in range(8):
                srcs = [xa[sl][:, 128 * (4 * g + j):128 * (4 * g + j) + 128]
                        for j in range(4)]
                oh = dsth[:, 4 * g:4 * g + 4, tt * 128:(tt + 1) * 128]
                ol = dstl[:, 4 * g:4 * g + 4, tt * 128:(tt + 1) * 128]
                _, _, evl = self.tp4(g % 4, srcs, oh, ol,
                                     tr_waits=[("va", xln_done[tt])])
                last_ev[(tt, g)] = evl
        return self.v("va"), self.v("sa")

    def p_a(self):
        base = self.base
        xlnt_h = self.sb("xlnt_h", [128, DC, TOK], BF16, base)
        xlnt_l = self.sb("xlnt_l", [128, DC, TOK], BF16, base + 65536)
        xa = [self.sb(f"xaA{i}", [128, D], F32, base + 131072 + i * 16384)
              for i in range(2)]
        nat = xa
        xb = [self.sb(f"xbA{i}", [128, D], F32, base + 163840 + i * 16384)
              for i in range(2)]
        kst_h = [self.sb(f"kst_h{i}", [128, TOK], BF16, base + 163840 + i * 2048)
                 for i in range(2)]
        kst_l = [self.sb(f"kst_l{i}", [128, TOK], BF16, base + 167936 + i * 2048)
                 for i in range(2)]
        vst = [self.sb(f"vst{i}", [128, TOK], F32, base + 172032 + i * 4096)
               for i in range(2)]
        wt_h = self.sb("wt_hA", [128, D], BF16, base + 180224)
        wt_l = self.sb("wt_lA", [128, D], BF16, base + 188416)

        xo = self.x_own.rearrange("(t p) d -> t p d", p=128)
        xs = self.x_sib.rearrange("(t p) d -> t p d", p=128)
        wth3 = wt_h[:].rearrange("p (a b) -> p a b", b=128)
        wtl3 = wt_l[:].rearrange("p (a b) -> p a b", b=128)

        accpair = [(4, 5), (6, 7)]
        nat_free = {}
        st_done = {}
        wt_ev = {}
        for tc in range(2):
            xsrc = [xo[tt] if tc == 0 else xs[tt] for tt in range(8)]
            pre = (self.all_dma("st") + [("pe", self.v("pe")),
                                         ("va", self.v("va")),
                                         ("sa", self.v("sa"))])
            xln_ready, xln_ready_h = self.ln_chunk(xsrc, xa, xb,
                                                   xlnt_h[:], xlnt_l[:],
                                                   pre_waits=pre)
            if tc == 0:
                mats = [("q", self.wq, self.qT_h, self.qT_l, None),
                        ("k", self.wk, self.kT_h, self.kT_l, None),
                        ("v", self.wv, None, None, self.v_f)]
            else:
                mats = [("k", self.wk, self.kT_h, self.kT_l, None),
                        ("v", self.wv, None, None, self.v_f)]
            for mi, (mn, wsrc, dh, dl, df) in enumerate(mats):
                for oc in range(DC):
                    goc = self._a_goc = getattr(self, "_a_goc", 0)
                    self._a_goc += 1
                    sl = goc % 2
                    natw = [("pe", nat_free.get(goc - 2, 0)),
                            ("va", xln_ready), ("sa", xln_ready_h)]
                    ldt = self.load(nat[sl][:], wsrc[oc * 128:(oc + 1) * 128, :],
                                    waits=natw)
                    for g in range(8):
                        srcs = [nat[sl][:, 128 * (4 * g + j):128 * (4 * g + j) + 128]
                                for j in range(4)]
                        osl = slice(g * 512, (g + 1) * 512)
                        pd, eh, el = self.tp4(
                            g % 4, srcs, wt_h[:, osl], wt_l[:, osl],
                            tr_waits=[ldt,
                                      ("pe", wt_ev.get("mm", 0))],
                            evh_waits=[("pe", wt_ev.get("mm", 0))],
                            evl_waits=[("pe", wt_ev.get("mm", 0))])
                        if g == 7:
                            nat_free[goc] = pd
                            wt_ready, wt_ready_h = el, eh
                    pair = goc % 2
                    b0, b1 = accpair[pair]

                    def mms(e, b0=b0, b1=b1):
                        a0, a1 = self.banks[b0], self.banks[b1]
                        for dc in range(DC):
                            first = dc == 0
                            last = dc == DC - 1
                            lh = wth3[:, dc, :]
                            ll = wtl3[:, dc, :]
                            e.matmul(a0, lh, xlnt_h[:, dc, 0:512], start=first,
                                     stop=False)
                            e.matmul(a1, lh, xlnt_h[:, dc, 512:1024], start=first,
                                     stop=False)
                            e.matmul(a0, lh, xlnt_l[:, dc, 0:512], start=False,
                                     stop=False)
                            e.matmul(a1, lh, xlnt_l[:, dc, 512:1024], start=False,
                                     stop=False)
                            e.matmul(a0, ll, xlnt_h[:, dc, 0:512], start=False,
                                     stop=last)
                            inst = e.matmul(a1, ll, xlnt_h[:, dc, 512:1024],
                                            start=False, stop=last)
                        return inst
                    mw = ([("va", wt_ready), ("sa", wt_ready_h)]
                          + self.bank_waits([b0, b1]))
                    md = self.emit("tensor", mms, waits=mw, inc="pe")
                    wt_ev["mm"] = md
                    stw = st_done.get(sl, [])
                    cs = slice(tc * TOK, (tc + 1) * TOK)
                    rs = slice(oc * 128, (oc + 1) * 128)
                    if df is None:
                        evh = self.emit(
                            "scalar",
                            lambda e, sl=sl, b0=b0, b1=b1: (
                                e.activation(out=kst_h[sl][:, 0:512],
                                             in_=self.banks[b0], func=AF.Copy),
                                e.activation(out=kst_h[sl][:, 512:1024],
                                             in_=self.banks[b1], func=AF.Copy))[-1],
                            waits=[("pe", md)] + stw, inc="sa")
                        evl = self.emit(
                            "vector",
                            lambda e, sl=sl, b0=b0, b1=b1: (
                                e.tensor_tensor(out=kst_l[sl][:, 0:512],
                                                in0=self.banks[b0],
                                                in1=kst_h[sl][:, 0:512],
                                                op=ALU.subtract),
                                e.tensor_tensor(out=kst_l[sl][:, 512:1024],
                                                in0=self.banks[b1],
                                                in1=kst_h[sl][:, 512:1024],
                                                op=ALU.subtract))[-1],
                            waits=[("sa", evh)] + stw, inc="va")
                        self.set_reader([b0, b1], "va", evl)
                        s1_ = self.store(dh[rs, cs], kst_h[sl][:], waits=[("sa", evh)])
                        s2_ = self.store(dl[rs, cs], kst_l[sl][:], waits=[("va", evl)])
                        st_done[sl] = [s1_, s2_]
                    else:
                        evf = self.emit(
                            "scalar",
                            lambda e, sl=sl, b0=b0, b1=b1: (
                                e.activation(out=vst[sl][:, 0:512],
                                             in_=self.banks[b0], func=AF.Copy),
                                e.activation(out=vst[sl][:, 512:1024],
                                             in_=self.banks[b1], func=AF.Copy))[-1],
                            waits=[("pe", md)] + stw, inc="sa")
                        self.set_reader([b0, b1], "sa", evf)
                        st_done[sl] = [self.store(df[rs, cs], vst[sl][:],
                                                  waits=[("sa", evf)])]
        self.a_store_done = self.all_dma("st")

    def p_b(self):
        base = self.base
        kh_h = [self.sb(f"kh_h{i}", [128, S], BF16, base + i * 4096) for i in range(2)]
        kh_l = [self.sb(f"kh_l{i}", [128, S], BF16, base + 8192 + i * 4096)
                for i in range(2)]
        vh = [self.sb(f"vh{i}", [128, S], F32, base + 16384 + i * 8192)
              for i in range(2)]
        qh_h = [self.sb(f"qh_h{i}", [128, TOK], BF16, base + 32768 + i * 2048)
                for i in range(2)]
        qh_l = [self.sb(f"qh_l{i}", [128, TOK], BF16, base + 36864 + i * 2048)
                for i in range(2)]
        vt_h = self.sb("vt_h", [128, S], BF16, base + 40960)
        vt_l = self.sb("vt_l", [128, S], BF16, base + 45056)
        P = [self.sb(f"Pbuf{i}", [128, S], F32, base + 49152 + i * 8192)
             for i in range(2)]
        pt_h = self.sb("pt_h", [128, 16, 512], BF16, base + 65536)
        pt_l = self.sb("pt_l", [128, 16, 512], BF16, base + 81920)
        yth = self.sb("yth", [128, 512], BF16, base + 98304)
        ytl = self.sb("ytl", [128, 512], BF16, base + 99328)
        vth3 = vt_h[:].rearrange("p (a b) -> p a b", b=128)
        vtl3 = vt_l[:].rearrange("p (a b) -> p a b", b=128)

        head_done = {}
        ydone = {"st": []}
        self._pbuf_tr = {}
        for h in range(NH):
            sl = h % 2
            rs = slice(h * 128, (h + 1) * 128)
            lw = self.a_store_done + [("pe", head_done.get(h - 2, 0))]
            ldtoks = [self.load(kh_h[sl][:], self.kT_h[rs, :], waits=lw),
                      self.load(kh_l[sl][:], self.kT_l[rs, :], waits=lw),
                      self.load(vh[sl][:], self.v_f[rs, :], waits=lw),
                      self.load(qh_h[sl][:], self.qT_h[rs, :], waits=lw),
                      self.load(qh_l[sl][:], self.qT_l[rs, :], waits=lw)]

            vt_prev = [("va", head_done.get(("vtev", h - 1), 0))]
            for g in range(4):
                srcs = [vh[sl][:, 128 * (4 * g + j):128 * (4 * g + j) + 128]
                        for j in range(4)]
                osl = slice(g * 512, (g + 1) * 512)
                _, eh, el = self.tp4(g % 4, srcs, vt_h[:, osl], vt_l[:, osl],
                                     tr_waits=ldtoks + vt_prev,
                                     evh_waits=vt_prev, evl_waits=vt_prev)
            head_done[("vtev", h)] = el
            vt_ready, vt_ready_h = el, eh

            pend = {}
            for qq in range(8):
                half = qq // 4
                qt = qq % 4
                qs = qt % 2
                st = [0, 1, 2, 3] if qq % 2 == 0 else [4, 5, 6, 7]
                big = self.psA if qq % 2 == 0 else self.psB

                def msc(e, sl=sl, qq=qq, st=st):
                    lh = qh_h[sl][:, qq * 128:(qq + 1) * 128]
                    ll = qh_l[sl][:, qq * 128:(qq + 1) * 128]
                    for kc in range(4):
                        pb = self.banks[st[kc]]
                        rh = kh_h[sl][:, kc * 512:(kc + 1) * 512]
                        rl = kh_l[sl][:, kc * 512:(kc + 1) * 512]
                        e.matmul(pb, lh, rh, start=True, stop=False)
                        e.matmul(pb, lh, rl, start=False, stop=False)
                        inst = e.matmul(pb, ll, rh, start=False, stop=True)
                    return inst
                scw = ([("va", vt_ready), ("sa", vt_ready_h)] + ldtoks
                       + self.bank_waits(st))
                sc_done = self.emit("tensor", msc, waits=scw, inc="pe")

                def vmax(e, qs=qs, big=big):
                    e.reduce_max(out=self.mrow[qs][:], in_=big[:, 0:2048],
                                 axis=AX.X)
                    e.drain()
                    return e.tensor_scalar(out=self.negm[qs][:],
                                           in0=self.mrow[qs][:], scalar1=-ISCALE,
                                           scalar2=None, op0=ALU.mult)
                mx_done = self.emit("vector", vmax,
                                    waits=[("pe", sc_done),
                                           ("va", head_done.get(("pn", h, qq - 2),
                                                                0))],
                                    inc="va")

                def sexp(e, qs=qs, qt=qt, big=big):
                    return e.activation(out=P[qt % 2][:], in_=big[:, 0:2048],
                                        func=AF.Exp, scale=ISCALE,
                                        bias=self.negm[qs][:],
                                        accum_out=self.zrow[qs][:])
                exp_done = self.emit(
                    "scalar", sexp,
                    waits=[("va", mx_done),
                           ("pe", self._pbuf_tr.get(qt % 2, 0))], inc="sa")
                self.set_reader(st, "sa", exp_done)

                def pnorm(e, qs=qs, qt=qt):
                    e.reciprocal(out=self.rz[qs][:], in_=self.zrow[qs][:])
                    e.drain()
                    return e.tensor_scalar(out=P[qt % 2][:], in0=P[qt % 2][:],
                                           scalar1=self.rz[qs][:], scalar2=None,
                                           op0=ALU.mult)
                pn_done = self.emit("vector", pnorm, waits=[("sa", exp_done)],
                                    inc="va")
                head_done[("pn", h, qq)] = pn_done
                pend[qq] = (qt, pn_done, st)

                def flush_tp(k):
                    qtk, pnk, stk = pend.pop(k)
                    pt_prev = head_done.get(("pv", h, k // 4 - 1),
                                            head_done.get(("pv", h - 1, 1), 0))
                    for g in range(4):
                        srcs = [P[qtk % 2][:,
                                128 * (4 * g + j):128 * (4 * g + j) + 128]
                                for j in range(4)]
                        oh = pt_h[:, 4 * g:4 * g + 4, qtk * 128:(qtk + 1) * 128]
                        ol = pt_l[:, 4 * g:4 * g + 4, qtk * 128:(qtk + 1) * 128]
                        pd2, eh2, el2 = self.tp4(stk[g], srcs, oh, ol,
                                                 tr_waits=[("va", pnk),
                                                           ("pe", pt_prev)],
                                                 evh_waits=[("pe", pt_prev)],
                                                 evl_waits=[("pe", pt_prev)])
                    self._pbuf_tr[qtk % 2] = pd2
                    return el2, eh2

                if qq - 1 in pend:
                    pt_ready, pt_ready_h = flush_tp(qq - 1)
                if qt == 3:
                    pt_ready, pt_ready_h = flush_tp(qq)

                    def mpv(e):
                        for kc in range(16):
                            first = kc == 0
                            last = kc == 15
                            lh = vth3[:, kc, :]
                            ll = vtl3[:, kc, :]
                            rh = pt_h[:, kc, :]
                            rl = pt_l[:, kc, :]
                            e.matmul(self.banks[3], lh, rh, start=first, stop=False)
                            e.matmul(self.banks[3], lh, rl, start=False, stop=False)
                            inst = e.matmul(self.banks[3], ll, rh, start=False,
                                            stop=last)
                        return inst
                    pvw = ([("va", pt_ready), ("sa", pt_ready_h)]
                           + self.bank_waits([3]))
                    pv_done = self.emit("tensor", mpv, waits=pvw, inc="pe")
                    head_done[("pv", h, half)] = pv_done
                    head_done[h] = pv_done

                    evh = self.emit(
                        "scalar",
                        lambda e: e.activation(out=yth[:], in_=self.banks[3],
                                               func=AF.Copy),
                        waits=[("pe", pv_done)] + ydone["st"], inc="sa")
                    evl = self.emit(
                        "vector",
                        lambda e: e.tensor_tensor(out=ytl[:], in0=self.banks[3],
                                                  in1=yth[:], op=ALU.subtract),
                        waits=[("sa", evh)] + ydone["st"], inc="va")
                    self.set_reader([3], "va", evl)
                    dsth = (self.attnT_h[half * 4:half * 4 + 4, :, rs]
                            .transpose([1, 0, 2]))
                    dstl = (self.attnT_l[half * 4:half * 4 + 4, :, rs]
                            .transpose([1, 0, 2]))
                    s1_ = self.store(dsth,
                                     yth[:].rearrange("p (a b) -> p a b", b=128),
                                     waits=[("sa", evh)])
                    s2_ = self.store(dstl,
                                     ytl[:].rearrange("p (a b) -> p a b", b=128),
                                     waits=[("va", evl)])
                    ydone["st"] = [s1_, s2_]
        self.b_store_done = self.all_dma("st")

    def p_c(self):
        base = self.base
        wot_h = self.sb("wot_h", [128, DC, 512], BF16, base)
        wot_l = self.sb("wot_l", [128, DC, 512], BF16, base + 32768)
        nat = [self.sb(f"natC{i}", [128, D], F32, base + 65536 + i * 16384)
               for i in range(2)]
        ath = [self.sb(f"ath{i}", [128, D], BF16, base + 98304 + i * 8192)
               for i in range(2)]
        atl = [self.sb(f"atl{i}", [128, D], BF16, base + 114688 + i * 8192)
               for i in range(2)]
        xsl = [self.sb(f"xslC{i}", [128, 512], F32, base + 131072 + i * 2048)
               for i in range(2)]
        x1st = [self.sb(f"x1stC{i}", [128, 512], F32, base + 135168 + i * 2048)
                for i in range(2)]

        mm_done = {}
        ev_done = {}
        st_done = {}
        nat_free = {}
        for og in range(8):
            last_mm_prev = mm_done.get((og - 1, 7), 0)
            for j in range(4):
                sl = j % 2
                ocr = og * 4 + j
                ldt = self.load(nat[sl][:], self.wo[ocr * 128:(ocr + 1) * 128, :],
                                waits=[("pe", nat_free.get((og, j - 2),
                                                           last_mm_prev))]
                                + self.b_store_done)
                for g in range(8):
                    srcs = [nat[sl][:, 128 * (4 * g + j2):128 * (4 * g + j2) + 128]
                            for j2 in range(4)]
                    oh = wot_h[:, 4 * g:4 * g + 4, j * 128:(j + 1) * 128]
                    ol = wot_l[:, 4 * g:4 * g + 4, j * 128:(j + 1) * 128]
                    pd, eh, el = self.tp4(g % 4, srcs, oh, ol,
                                          tr_waits=[ldt,
                                                    ("pe", last_mm_prev)],
                                          evh_waits=[("pe", last_mm_prev)],
                                          evl_waits=[("pe", last_mm_prev)])
                    if g == 7:
                        nat_free[(og, j)] = pd
            wot_ready, wot_ready_h = self.v("va"), self.v("sa")

            for tt in range(8):
                sl = tt % 2
                lw = self.b_store_done + [("pe", mm_done.get((og, tt - 2), 0))]
                t_ah = self.load(ath[sl][:], self.attnT_h[tt], waits=lw)
                t_al = self.load(atl[sl][:], self.attnT_l[tt], waits=lw)
                ldx = self.load(xsl[sl][:],
                                self.x_own[tt * 128:(tt + 1) * 128,
                                           og * 512:(og + 1) * 512],
                                waits=[("va", ev_done.get((og, tt - 2), 0))])
                ath3 = ath[sl][:].rearrange("p (a b) -> p a b", b=128)
                atl3 = atl[sl][:].rearrange("p (a b) -> p a b", b=128)
                bk = 4 + tt % 2

                def mmc(e, ath3=ath3, atl3=atl3, bk=bk):
                    pb = self.banks[bk]
                    for ec in range(DC):
                        first = ec == 0
                        last = ec == DC - 1
                        lh = ath3[:, ec, :]
                        ll = atl3[:, ec, :]
                        rh = wot_h[:, ec, :]
                        rl = wot_l[:, ec, :]
                        e.matmul(pb, lh, rh, start=first, stop=False)
                        e.matmul(pb, lh, rl, start=False, stop=False)
                        inst = e.matmul(pb, ll, rh, start=False, stop=last)
                    return inst
                mw = ([("va", wot_ready), ("sa", wot_ready_h), t_ah, t_al]
                      + self.bank_waits([bk]))
                md = self.emit("tensor", mmc, waits=mw, inc="pe")
                mm_done[(og, tt)] = md

                def evc(e, sl=sl, bk=bk, tt=tt, og=og):
                    return e.scalar_tensor_tensor(
                        out=x1st[sl][:], in0=self.banks[bk], scalar=1.0,
                        in1=xsl[sl][:], op0=ALU.mult, op1=ALU.add,
                        accum_out=self.parts[:, tt * 8 + og:tt * 8 + og + 1])
                ed = self.emit("vector", evc,
                               waits=[("pe", md), ldx] + st_done.get(sl, []),
                               inc="va")
                ev_done[(og, tt)] = ed
                self.set_reader([bk], "va", ed)
                st_done[sl] = [self.store(
                    self.x1[tt * 128:(tt + 1) * 128, og * 512:(og + 1) * 512],
                    x1st[sl][:], waits=[("va", ed)])]
        self.c_store_done = self.all_dma("st")

    def p_c2(self):
        base = self.base
        self.xln2t_h = self.sb("xln2t_h", [128, DC, TOK], BF16, base)
        self.xln2t_l = self.sb("xln2t_l", [128, DC, TOK], BF16, base + 65536)
        xa = [self.sb(f"xaC2{i}", [128, D], F32, base + 131072 + i * 16384)
              for i in range(2)]
        xb = [self.sb(f"xbC2{i}", [128, D], F32, base + 163840 + i * 16384)
              for i in range(2)]
        x1v = self.x1.rearrange("(t p) d -> t p d", p=128)
        pre = self.c_store_done + [("pe", self.v("pe")),
                                   ("va", self.v("va")), ("sa", self.v("sa"))]
        self.xln2_ready, self.xln2_ready_h = self.ln_chunk(
            [x1v[tt] for tt in range(8)], xa, xb, self.xln2t_h[:], self.xln2t_l[:],
            pre_waits=pre,
            sum_from_parts=lambda tt: self.parts[:, tt * 8:tt * 8 + 8])

    def p_d(self):
        base = self.base + 131072
        pk1 = self.sb("pkD1", [128, 2048], I32, base)
        pk3 = self.sb("pkD3", [128, 2048], I32, base + 8192)
        sc1 = self.sb("scD1", [128, 64], F32, base + 16384)
        sc3 = self.sb("scD3", [128, 64], F32, base + 16640)
        msb = self.sb("msbD", [128, 256], I32, base + 16896)
        lsb = self.sb("lsbD", [128, 256], I32, base + 17920)
        wn = [[self.sb(f"wnD{m}_{s}", [128, 4, 128], F32,
                       base + 18944 + (m * 4 + s) * 2048) for s in range(4)]
              for m in range(2)]
        wn_extra = 8192
        wt_h = [self.sb(f"wtDh{m}", [128, D], BF16, base + 35328 + m * 8192)
                for m in range(2)]
        wt_l = [self.sb(f"wtDl{m}", [128, D], BF16, base + 51712 + m * 8192)
                for m in range(2)]
        t1 = self.sb("t1D", [128, TOK], F32, base + 68096)
        gf = self.sb("gfD", [128, TOK], F32, base + 72192)
        gh = self.sb("ghD", [128, TOK], BF16, base + 76288)
        gl = self.sb("glD", [128, TOK], BF16, base + 78336)

        pkv1 = self.w1.rearrange("(h i) c -> h (i c)", i=DC)
        pkv3 = self.w3.rearrange("(h i) c -> h (i c)", i=DC)
        scv1 = self.s1.rearrange("(h g) -> h g", g=64)
        scv3 = self.s3.rearrange("(h g) -> h g", g=64)

        mm_done = {}
        ext_done = {}
        tp_read = {}
        st_prev = []
        for hc in range(HC):
            lw = [("va", ext_done.get((hc - 1, 1, 7), 0)),
                  ("va", self.xln2_ready), ("sa", self.xln2_ready_h)]
            self.load(pk1[:], pkv1[hc * 128:(hc + 1) * 128, :], waits=lw)
            self.load(sc1[:], scv1[hc * 128:(hc + 1) * 128, :], waits=lw)
            self.load(pk3[:], pkv3[hc * 128:(hc + 1) * 128, :], waits=lw)
            ldt = self.load(sc3[:], scv3[hc * 128:(hc + 1) * 128, :], waits=lw)

            seta = hc % 2
            acc = [0, 1, 2, 3] if seta == 0 else [4, 5, 6, 7]
            oth = [4, 5, 6, 7] if seta == 0 else [0, 1, 2, 3]

            last_eh = last_el = 0
            tpi = 0
            for qq in range(8):
                for m in range(2):
                    pkm = pk1 if m == 0 else pk3
                    scm = sc1 if m == 0 else sc3
                    ws = qq % 4

                    def ext(e, pkm=pkm, scm=scm, m=m, qq=qq, ws=ws):
                        pks = pkm[:, qq * 256:(qq + 1) * 256]
                        e.tensor_scalar(out=msb[:], in0=pks, scalar1=4, scalar2=None,
                                        op0=ALU.arith_shift_right)
                        e.tensor_scalar(out=lsb[:], in0=pks, scalar1=28, scalar2=28,
                                        op0=ALU.arith_shift_left,
                                        op1=ALU.arith_shift_right)
                        s_m = (scm[:, 8 * qq:8 * qq + 8:2].unsqueeze(-1)
                               .broadcast_to([128, 4, 64]))
                        s_l = (scm[:, 8 * qq + 1:8 * qq + 8:2].unsqueeze(-1)
                               .broadcast_to([128, 4, 64]))
                        e.tensor_tensor(out=wn[m][ws][:, :, 0:64],
                                        in0=msb[:].rearrange("p (i c) -> p i c", c=64),
                                        in1=s_m, op=ALU.mult)
                        return e.tensor_tensor(
                            out=wn[m][ws][:, :, 64:128],
                            in0=lsb[:].rearrange("p (i c) -> p i c", c=64),
                            in1=s_l, op=ALU.mult)
                    exw = [ldt, ("pe", tp_read.get((hc, m, qq - 4),
                                                   tp_read.get((hc - 1, m, qq + 4),
                                                               0)))]
                    ext_done[(hc, m, qq)] = self.emit("vector", ext, waits=exw,
                                                      inc="va")
                    dc0 = qq * 4
                    srcs = [wn[m][ws][:, j, :] for j in range(4)]
                    tpb = oth[tpi % 4]
                    tpi += 1
                    guard = [("pe", mm_done.get(hc - 1, 0))]
                    oh = wt_h[m][:, dc0 * 128:(dc0 + 4) * 128]
                    ol = wt_l[m][:, dc0 * 128:(dc0 + 4) * 128]
                    pd, eh, el = self.tp4(
                        tpb, srcs, oh, ol,
                        tr_waits=[("va", ext_done[(hc, m, qq)])] + guard,
                        evh_waits=guard, evl_waits=guard)
                    last_eh, last_el = eh, el
                    tp_read[(hc, m, qq)] = pd

            def mmd(e, acc=acc):
                a0, a1 = self.banks[acc[0]], self.banks[acc[1]]
                b0, b1 = self.banks[acc[2]], self.banks[acc[3]]
                w1h3 = wt_h[0][:].rearrange("p (a b) -> p a b", b=128)
                w1l3 = wt_l[0][:].rearrange("p (a b) -> p a b", b=128)
                w3h3 = wt_h[1][:].rearrange("p (a b) -> p a b", b=128)
                w3l3 = wt_l[1][:].rearrange("p (a b) -> p a b", b=128)
                for dc in range(DC):
                    first = dc == 0
                    last = dc == DC - 1
                    xh0 = self.xln2t_h[:, dc, 0:512]
                    xh1 = self.xln2t_h[:, dc, 512:1024]
                    xl0 = self.xln2t_l[:, dc, 0:512]
                    xl1 = self.xln2t_l[:, dc, 512:1024]
                    for (wh3, wl3, c0, c1) in ((w1h3, w1l3, a0, a1),
                                               (w3h3, w3l3, b0, b1)):
                        lh = wh3[:, dc, :]
                        ll = wl3[:, dc, :]
                        e.matmul(c0, lh, xh0, start=first, stop=False)
                        e.matmul(c1, lh, xh1, start=first, stop=False)
                        e.matmul(c0, lh, xl0, start=False, stop=False)
                        e.matmul(c1, lh, xl1, start=False, stop=False)
                        e.matmul(c0, ll, xh0, start=False, stop=last)
                        inst = e.matmul(c1, ll, xh1, start=False, stop=last)
                return inst
            mw = ([("sa", last_eh), ("va", last_el)] + self.bank_waits(acc))
            md = self.emit("tensor", mmd, waits=mw, inc="pe")
            mm_done[hc] = md

            def silu(e, acc=acc):
                big = self.psA if acc[0] == 0 else self.psB
                return e.activation(out=t1[:], in_=big[:, 0:1024], func=AF.Silu)
            sd = self.emit("scalar", silu, waits=[("pe", md)], inc="sa")
            self.set_reader(acc[0:2], "sa", sd)

            def gmul(e, acc=acc):
                big = self.psA if acc[0] == 0 else self.psB
                return e.tensor_tensor(out=gf[:], in0=big[:, 1024:2048],
                                       in1=t1[:], op=ALU.mult)
            gd = self.emit("vector", gmul, waits=[("sa", sd)], inc="va")
            self.set_reader(acc[2:4], "va", gd)

            hd = self.emit("scalar",
                           lambda e: e.activation(out=gh[:], in_=gf[:], func=AF.Copy),
                           waits=[("va", gd)] + st_prev, inc="sa")
            ldn = self.emit("vector",
                            lambda e: e.tensor_tensor(out=gl[:], in0=gf[:], in1=gh[:],
                                                      op=ALU.subtract),
                            waits=[("sa", hd)] + st_prev, inc="va")
            s1_ = self.store(self.g_h[hc], gh[:], waits=[("sa", hd)])
            s2_ = self.store(self.g_l[hc], gl[:], waits=[("va", ldn)])
            st_prev = [s1_, s2_]
        self.d_store_done = self.all_dma("st")

    def p_e(self):
        base = self.base
        w2b_h = [self.sb(f"w2b_h{i}", [128, 512], BF16, base + i * 1024)
                 for i in range(2)]
        w2b_l = [self.sb(f"w2b_l{i}", [128, 512], BF16, base + 2048 + i * 1024)
                 for i in range(2)]
        gbh = [self.sb(f"gbh{i}", [128, TOK], BF16, base + 4096 + i * 2048)
               for i in range(2)]
        gbl = [self.sb(f"gbl{i}", [128, TOK], BF16, base + 8192 + i * 2048)
               for i in range(2)]
        x1b = [self.sb(f"x1bE{i}", [128, 512], F32, base + 12288 + i * 2048)
               for i in range(2)]
        outst = [self.sb(f"outstE{i}", [128, 512], F32, base + 16384 + i * 2048)
                 for i in range(2)]

        ev_done = {}
        mm_prog = {}
        so_done = {}
        for og in range(8):
            bank_free = self.bank_waits(list(range(8)))
            for hc in range(HC):
                sl = hc % 2
                lw = self.d_store_done + [("pe", mm_prog.get((og, hc - 2), 0))]
                if hc < 2:
                    lw = lw + [("va", ev_done.get((og - 1, "all"), 0))]
                etoks = [self.load(w2b_h[sl][:],
                                   self.w2t_h[hc][:, og * 512:(og + 1) * 512],
                                   waits=lw),
                         self.load(w2b_l[sl][:],
                                   self.w2t_l[hc][:, og * 512:(og + 1) * 512],
                                   waits=lw),
                         self.load(gbh[sl][:], self.g_h[hc], waits=lw),
                         self.load(gbl[sl][:], self.g_l[hc], waits=lw)]

                def mme(e, sl=sl, hc=hc):
                    first = hc == 0
                    last = hc == HC - 1
                    for tt in range(8):
                        lh = gbh[sl][:, tt * 128:(tt + 1) * 128]
                        ll = gbl[sl][:, tt * 128:(tt + 1) * 128]
                        pb = self.banks[tt]
                        e.matmul(pb, lh, w2b_h[sl][:], start=first, stop=False)
                        e.matmul(pb, lh, w2b_l[sl][:], start=False, stop=False)
                        inst = e.matmul(pb, ll, w2b_h[sl][:], start=False, stop=last)
                    return inst
                mw = etoks + (bank_free if hc == 0 else [])
                md = self.emit("tensor", mme, waits=mw, inc="pe")
                mm_prog[(og, hc)] = md
            og_done = mm_prog[(og, HC - 1)]
            for tt in range(8):
                sl = tt % 2
                ldx = self.load(x1b[sl][:],
                                self.x1[tt * 128:(tt + 1) * 128,
                                        og * 512:(og + 1) * 512],
                                waits=self.c_store_done
                                + [("va", ev_done.get((og, tt - 2), 0))])

                def eve(e, sl=sl, tt=tt):
                    return e.scalar_tensor_tensor(
                        out=outst[sl][:], in0=self.banks[tt], scalar=1.0,
                        in1=x1b[sl][:], op0=ALU.mult, op1=ALU.add)
                ed = self.emit("vector", eve,
                               waits=[("pe", og_done), ldx] + so_done.get(sl, []),
                               inc="va")
                ev_done[(og, tt)] = ed
                self.set_reader([tt], "va", ed)
                so_done[sl] = [self.store(
                    self.out[tt * 128:(tt + 1) * 128, og * 512:(og + 1) * 512],
                    outst[sl][:], waits=[("va", ed)], pool="so")]
        self.emit("scalar", lambda e: e.nop(), waits=self.all_dma("so"))

    def build(self):
        self.p_init()
        if self.upto >= 1:
            self.p_prep()
        else:
            self.prep_store_done = []
        if self.upto >= 2:
            self.p_a()
        if self.upto >= 3:
            self.p_b()
        if self.upto >= 4:
            self.p_c()
        if self.upto >= 5:
            self.p_c2()
        if self.upto >= 6:
            self.p_d()
        if self.upto >= 7:
            self.p_e()
        else:
            self.emit("scalar", lambda e: e.nop(),
                      waits=self.all_dma("st") + self.all_dma("ld"))
        return self.finalize()


_NC_CACHE = {}


def get_nc(dbg=False, upto=9):
    key = (dbg, upto)
    if key not in _NC_CACHE:
        _NC_CACHE[key] = Builder(dbg=dbg, upto=upto).build()
    return _NC_CACHE[key]


def make_in_maps(x, wq, wk, wv, wo, w1, s1, w2, s2, w3, s3):
    in_maps = []
    for c in range(8):
        b, hf = c // 2, c % 2
        in_maps.append({
            "x_own": np.ascontiguousarray(x[b, hf * TOK:(hf + 1) * TOK]),
            "x_sib": np.ascontiguousarray(x[b, (1 - hf) * TOK:(2 - hf) * TOK]),
            "wq": wq, "wk": wk, "wv": wv, "wo": wo,
            "w1": w1, "w2": w2, "w3": w3, "s1": s1, "s2": s2, "s3": s3,
        })
    return in_maps


def kernel(L0_attn_x, L0_attn_w_q, L0_attn_w_k, L0_attn_w_v, L0_attn_w_o,
           L0_ffn_w1, L0_ffn_s1, L0_ffn_w2, L0_ffn_s2, L0_ffn_w3, L0_ffn_s3):
    asc = np.ascontiguousarray
    x = asc(np.asarray(L0_attn_x, dtype=np.float32))
    args = (asc(np.asarray(L0_attn_w_q, dtype=np.float32)),
            asc(np.asarray(L0_attn_w_k, dtype=np.float32)),
            asc(np.asarray(L0_attn_w_v, dtype=np.float32)),
            asc(np.asarray(L0_attn_w_o, dtype=np.float32)),
            asc(np.asarray(L0_ffn_w1, dtype=np.int32)),
            asc(np.asarray(L0_ffn_s1, dtype=np.float32)),
            asc(np.asarray(L0_ffn_w2, dtype=np.int32)),
            asc(np.asarray(L0_ffn_s2, dtype=np.float32)),
            asc(np.asarray(L0_ffn_w3, dtype=np.int32)),
            asc(np.asarray(L0_ffn_s3, dtype=np.float32)))
    nc = get_nc()
    in_maps = make_in_maps(x, *args)
    res = run_bass_kernel_spmd(nc, in_maps, core_ids=list(range(8)))
    y = np.empty((4, S, D), dtype=np.float32)
    for c in range(8):
        b, hf = c // 2, c % 2
        y[b, hf * TOK:(hf + 1) * TOK] = res.results[c]["out"]
    return y


# version 08 (final)
# speedup vs baseline: 1.1116x; 1.1116x over previous
import numpy as np
import concourse.bass as bass
import concourse.mybir as mybir
from concourse.bass_utils import run_bass_kernel_spmd

F32 = mybir.dt.float32
BF16 = mybir.dt.bfloat16
I32 = mybir.dt.int32
AF = mybir.ActivationFunctionType
ALU = mybir.AluOpType
AX = mybir.AxisListType

D = 4096
DC = D // 128
TOK = 1024
S = 2048
HID = 11008
HC = HID // 128
NH = 32
ISCALE = 1.0 / float(np.sqrt(128.0))

ENGINES = ("sync", "scalar", "vector", "tensor", "gpsimd")
LIMITS = {}
SBUF_LIMIT = 212800
_DT_SIZE = {F32: 4, BF16: 2, I32: 4}


class Builder:
    def __init__(self, dbg=False, upto=9, race_detect=False):
        self.dbg = dbg
        self.upto = upto
        self.nc = bass.Bass("TRN2", target_bir_lowering=False, debug=False,
                            detect_race_conditions=race_detect)
        self.ops = {e: [] for e in ENGINES}
        self.sems = {}
        self.cnt = {}
        for s in ("pe", "va", "sa", "ga"):
            self.sem(s)
        self._rr = {}
        for pool, n in (("ld", 8), ("st", 8), ("so", 4)):
            self._rr[pool] = 0
            for k in range(n):
                self.sem(f"{pool}{k}")
        self._pool_n = {"ld": 8, "st": 8, "so": 4}
        self._declare_dram()
        psA = self.nc.alloc_psum_tensor("psA", [128, 2048], F32)
        psB = self.nc.alloc_psum_tensor("psB", [128, 2048], F32)
        self.banks = [psA[:, 0:512], psA[:, 512:1024], psA[:, 1024:1536],
                      psA[:, 1536:2048], psB[:, 0:512], psB[:, 512:1024],
                      psB[:, 1024:1536], psB[:, 1536:2048]]
        self.psA = psA
        self.psB = psB
        self.bank_last = {i: ("va", 0) for i in range(8)}
        self._sbuf_off = 0
        self._persistent_sbuf()

    def sem(self, name):
        if name not in self.sems:
            self.sems[name] = self.nc.alloc_semaphore(name)
            self.cnt[name] = 0
        return name

    def emit(self, eng, fn, waits=(), inc=None, inc_n=1):
        wl = []
        seen = {}
        for s, v in waits:
            assert v <= self.cnt[s], f"wait on future inc: {s} {v} > {self.cnt[s]}"
            if v > 0 and seen.get(s, -1) < v:
                seen[s] = v
        wl = [(self.sems[s], v) for s, v in seen.items()]
        if inc is not None:
            self.cnt[inc] += inc_n
        sem_h = self.sems[inc] if inc is not None else None

        def run(e, fn=fn, wl=wl, sem_h=sem_h, inc_n=inc_n, eng=eng):
            for h, v in wl:
                e.wait_ge(h, v)
            inst = fn(e)
            if sem_h is not None:
                if inst is None:
                    if eng == "tensor":
                        e.drain()
                    inst = e.nop()
                inst.then_inc(sem_h, inc_n)
        self.ops[eng].append(run)
        return self.cnt[inc] if inc is not None else None

    def v(self, sem):
        return self.cnt[sem]

    def _dma(self, eng, pool, out, in_, waits):
        k = self._rr[pool] % self._pool_n[pool]
        self._rr[pool] += 1
        sem = f"{pool}{k}"
        w = list(waits) + [(sem, self.cnt[sem])]
        val = self.emit(eng, lambda e, o=out, i=in_: e.dma_start(out=o, in_=i),
                        waits=w, inc=sem, inc_n=16)
        return (sem, val)

    def load(self, out, in_, waits=()):
        return self._dma("sync", "ld", out, in_, waits)

    def store(self, out, in_, waits=(), pool="st"):
        return self._dma("scalar", pool, out, in_, waits)

    def all_dma(self, pool):
        return [(f"{pool}{k}", self.cnt[f"{pool}{k}"])
                for k in range(self._pool_n[pool])]

    def sb(self, name, shape, dtype, offset):
        per_part = int(np.prod(shape[1:])) * _DT_SIZE[dtype]
        assert offset + per_part <= SBUF_LIMIT, f"SBUF overflow: {name} @{offset}+{per_part}"
        origin = 16512
        assert origin >= self.nc.sbuf_base
        assert origin + offset + per_part <= self.nc.sbuf_top
        return self.nc.alloc_sbuf_tensor_at(name, shape, dtype, offset=origin + offset)

    def finalize(self):
        nc = self.nc
        with nc.Block() as block:
            for e in ENGINES:
                lst = self.ops[e]
                if lst:
                    getattr(block, e)(lambda eng, lst=lst: [f(eng) for f in lst])
        for s, c in self.cnt.items():
            assert c < 60000, f"sem {s} count {c} too high"
        return nc

    def bank_waits(self, bks):
        return [self.bank_last[b] for b in bks]

    def set_reader(self, bks, sem, val):
        for b in bks:
            old = self.bank_last[b]
            if old[0] == sem:
                self.bank_last[b] = (sem, max(old[1], val))
            else:
                self.bank_last[b] = (sem, val)

    def tp4(self, bank, srcs, outh, outl, tr_waits=(), evh_waits=(), evl_waits=()):
        n = len(srcs)
        pb = self.banks[bank]
        w = list(tr_waits) + self.bank_waits([bank]) + [("va", self.ident_ready)]

        def tr(e, srcs=srcs, pb=pb):
            inst = None
            for j, s in enumerate(srcs):
                inst = e.transpose(pb[:, 128 * j:128 * j + 128], s, self.ident_f[:])
            return inst
        pdone = self.emit("tensor", tr, waits=w, inc="pe")
        if len(outh.shape) == 2:
            outh = outh.rearrange("p (a b) -> p a b", b=128)
        if len(outl.shape) == 2:
            outl = outl.rearrange("p (a b) -> p a b", b=128)
        pv = pb[:, 0:n * 128].rearrange("p (a b) -> p a b", b=128)
        evh = self.emit("scalar",
                        lambda e, outh=outh, pv=pv: e.activation(out=outh, in_=pv,
                                                                 func=AF.Copy),
                        waits=[("pe", pdone)] + list(evh_waits), inc="sa")
        evl = self.emit("vector",
                        lambda e, outl=outl, outh=outh, pv=pv: e.tensor_tensor(
                            out=outl, in0=pv, in1=outh, op=ALU.subtract),
                        waits=[("sa", evh)] + list(evl_waits), inc="va")
        self.set_reader([bank], "va", evl)
        return pdone, evh, evl

    def _declare_dram(self):
        nc = self.nc
        dbgkind = "ExternalOutput" if self.dbg else "Internal"
        T = lambda n, s, d, k: nc.dram_tensor(n, s, d, kind=k).ap()
        self.x_own = T("x_own", [TOK, D], F32, "ExternalInput")
        self.x_sib = T("x_sib", [TOK, D], F32, "ExternalInput")
        self.wq = T("wq", [D, D], F32, "ExternalInput")
        self.wk = T("wk", [D, D], F32, "ExternalInput")
        self.wv = T("wv", [D, D], F32, "ExternalInput")
        self.wo = T("wo", [D, D], F32, "ExternalInput")
        self.w1 = T("w1", [352256, 64], I32, "ExternalInput")
        self.w2 = T("w2", [352256, 64], I32, "ExternalInput")
        self.w3 = T("w3", [352256, 64], I32, "ExternalInput")
        self.s1 = T("s1", [704512], F32, "ExternalInput")
        self.s2 = T("s2", [704512], F32, "ExternalInput")
        self.s3 = T("s3", [704512], F32, "ExternalInput")
        self.out = T("out", [TOK, D], F32, "ExternalOutput")
        self.w1t_h = T("w1t_h", [HC, 128, DC * 128], BF16, "Internal")
        self.w1t_l = T("w1t_l", [HC, 128, DC * 128], BF16, "Internal")
        self.w3t_h = T("w3t_h", [HC, 128, DC * 128], BF16, "Internal")
        self.w3t_l = T("w3t_l", [HC, 128, DC * 128], BF16, "Internal")
        self.w2t_h = T("w2t_h", [HC, 128, D], BF16, dbgkind)
        self.w2t_l = T("w2t_l", [HC, 128, D], BF16, "Internal")
        self.kT_h = T("kT_h", [D, S], BF16, dbgkind)
        self.kT_l = T("kT_l", [D, S], BF16, "Internal")
        self.qT_h = T("qT_h", [D, TOK], BF16, dbgkind)
        self.qT_l = T("qT_l", [D, TOK], BF16, "Internal")
        self.v_f = T("v_f", [D, S], F32, dbgkind)
        self.attnT_h = T("attnT_h", [8, 128, D], BF16, dbgkind)
        self.attnT_l = T("attnT_l", [8, 128, D], BF16, "Internal")
        self.x1 = T("x1", [TOK, D], F32, dbgkind)
        self.g_h = T("g_h", [HC, 128, TOK], BF16, dbgkind)
        self.g_l = T("g_l", [HC, 128, TOK], BF16, "Internal")

    def _persistent_sbuf(self):
        off = 0

        def nxt(name, shape, dtype):
            nonlocal off
            t = self.sb(name, shape, dtype, off)
            per_part = int(np.prod(shape[1:])) * _DT_SIZE[dtype]
            off += ((per_part + 31) // 32) * 32
            return t
        self.ident_f = nxt("ident_f", [128, 128], F32)
        self.eps1 = nxt("eps1", [128, 1], F32)
        self.sx = [nxt(f"sx{i}", [128, 1], F32) for i in range(2)]
        self.mean = [nxt(f"mean{i}", [128, 1], F32) for i in range(2)]
        self.sxc2 = [nxt(f"sxc2{i}", [128, 1], F32) for i in range(2)]
        self.sdv = [nxt(f"sdv{i}", [128, 1], F32) for i in range(2)]
        self.inv = [nxt(f"inv{i}", [128, 1], F32) for i in range(2)]
        self.mrow = [nxt(f"mrow{i}", [128, 1], F32) for i in range(2)]
        self.negm = [nxt(f"negm{i}", [128, 1], F32) for i in range(2)]
        self.zrow = self.sdv
        self.rz = [nxt(f"rz{i}", [128, 1], F32) for i in range(2)]
        self.parts = nxt("parts", [128, 64], F32)
        self.base = (off + 63) // 64 * 64
        self.ident_i = self.sb("ident_i", [128, 128], I32, self.base)

    def p_init(self):
        self.emit("gpsimd",
                  lambda e: e.iota(self.ident_i[:, :], [[1, 128]], base=0,
                                   channel_multiplier=-1),
                  inc="ga")
        self.ident_ready = None
        iv = self.emit(
            "vector",
            lambda e: e.tensor_scalar(out=self.ident_f[:], in0=self.ident_i[:],
                                      scalar1=0.0, scalar2=None, op0=ALU.is_equal),
            waits=[("ga", self.v("ga"))], inc="va")
        self.emit("vector", lambda e: e.memset(self.eps1[:], 1e-5), inc="va")
        self.ident_ready = self.v("va")
        self.emit("tensor",
                  lambda e: e.matmul(self.banks[7][:, 0:128], self.ident_f[:],
                                     self.ident_f[:], start=True, stop=True),
                  waits=[("va", self.ident_ready)], inc="pe")

    def _prep_one(self, pkv, scv, dsth, dstl, nrows, ngroups, pk, scb, msb, lsb, wn,
                  stg_h, stg_l, state, stg_h2=None, stg_l2=None):
        nw = ngroups * 64
        wnf = wn.rearrange("p a b -> p (a b)")
        for rc in range(nrows):
            gr = state.get("row", 0)
            state["row"] = gr + 1
            sl = gr % 2
            w = [("va", state["ext"].get(gr - 2, 0))]
            t0 = self.load(pk[sl][:, 0:nw], pkv[rc * 128:(rc + 1) * 128, :], waits=w)
            t1 = self.load(scb[sl][:, 0:2 * ngroups],
                           scv[rc * 128:(rc + 1) * 128, :], waits=w)

            def ext(e, sl=sl, ng=ngroups):
                e.tensor_scalar(out=msb[:, 0:nw], in0=pk[sl][:, 0:nw], scalar1=4,
                                scalar2=None, op0=ALU.arith_shift_right)
                e.tensor_scalar(out=lsb[:, 0:nw], in0=pk[sl][:, 0:nw], scalar1=28,
                                scalar2=28, op0=ALU.arith_shift_left,
                                op1=ALU.arith_shift_right)
                s_m = scb[sl][:, 0:2 * ng:2].unsqueeze(-1).broadcast_to([128, ng, 64])
                s_l = scb[sl][:, 1:2 * ng:2].unsqueeze(-1).broadcast_to([128, ng, 64])
                e.tensor_tensor(out=wn[:, :, 0:64],
                                in0=msb[:, 0:nw].rearrange("p (i c) -> p i c", c=64),
                                in1=s_m, op=ALU.mult)
                return e.tensor_tensor(
                    out=wn[:, :, 64:128],
                    in0=lsb[:, 0:nw].rearrange("p (i c) -> p i c", c=64),
                    in1=s_l, op=ALU.mult)
            exw = [t0, t1, ("pe", state["tp"].get(gr - 1, 0))]
            state["ext"][gr] = self.emit("vector", ext, waits=exw, inc="va")

            if stg_h2 is not None and sl == 1:
                cur_h, cur_l, skey = stg_h2, stg_l2, 1
            else:
                cur_h, cur_l, skey = stg_h, stg_l, 0
            last_pd = 0
            ntp = (ngroups + 3) // 4
            for g in range(ntp):
                n = min(4, ngroups - 4 * g)
                bank = g % 4
                srcs = [wnf[:, 128 * (4 * g + j):128 * (4 * g + j) + 128]
                        for j in range(n)]
                osl = slice(g * 512, g * 512 + n * 128)
                stoks = state["stored"].get(skey, [])
                pd, eh, el = self.tp4(
                    bank, srcs, cur_h[:, osl], cur_l[:, osl],
                    tr_waits=[("va", state["ext"][gr])],
                    evh_waits=stoks, evl_waits=stoks)
                last_pd = pd
                last_eh, last_el = eh, el
            state["tp"][gr] = last_pd
            oh = dsth(rc)
            ol = dstl(rc)
            ih = cur_h[:, 0:ngroups * 128]
            il = cur_l[:, 0:ngroups * 128]
            if len(oh.shape) == 3:
                ih = ih.rearrange("p (a b) -> p a b", b=128)
                il = il.rearrange("p (a b) -> p a b", b=128)
            s1_ = self.store(oh, ih, waits=[("sa", last_eh)])
            s2_ = self.store(ol, il, waits=[("va", last_el)])
            state["stored"][skey] = [s1_, s2_]

    def p_prep(self):
        base = self.base
        pk = [self.sb(f"pk{i}", [128, 5504], I32, base + i * 22016) for i in range(2)]
        scb = [self.sb(f"scb{i}", [128, 172], F32, base + 44032 + i * 704)
               for i in range(2)]
        msb = self.sb("msb_p", [128, 5504], I32, base + 45440)
        lsb = self.sb("lsb_p", [128, 5504], I32, base + 67456)
        wn = self.sb("wn_p", [128, HC, 128], F32, base + 89472)
        stg_h = self.sb("pstg_h", [128, HC * 128], BF16, base + 133504)
        stg_l = self.sb("pstg_l", [128, HC * 128], BF16, base + 155520)

        st13 = {"ext": {}, "tp": {}, "stored": {}}

        pkv2 = self.w2.rearrange("(o i) c -> o (i c)", i=HC)
        scv2 = self.s2.rearrange("(o g) -> o g", g=172)

        def d2h(oc):
            return self.w2t_h[0:HC, :, oc * 128:(oc + 1) * 128].transpose([1, 0, 2])

        def d2l(oc):
            return self.w2t_l[0:HC, :, oc * 128:(oc + 1) * 128].transpose([1, 0, 2])
        if not LIMITS.get("skip_w2"):
            self._prep_one(pkv2, scv2, d2h, d2l, LIMITS.get("w2_rows", DC), HC,
                           pk, scb, msb, lsb, wn.ap(), stg_h[:], stg_l[:], st13)
        self.prep_store_done = self.all_dma("st")


    def _prep13_gen(self):
        base = self.base + 100352
        pk = [self.sb(f"pk13_{i}", [128, 2048], I32, base + i * 8192)
              for i in range(2)]
        scb = [self.sb(f"scb13_{i}", [128, 64], F32, base + 16384 + i * 256)
               for i in range(2)]
        msb = self.sb("msb13", [128, 2048], I32, base + 16896)
        lsb = self.sb("lsb13", [128, 2048], I32, base + 25088)
        wn = self.sb("wn13", [128, DC, 128], F32, base + 33280)
        stg_h = [self.sb(f"stg13h{i}", [128, DC * 128], BF16,
                         base + 49664 + i * 8192) for i in range(2)]
        stg_l = [self.sb(f"stg13l{i}", [128, DC * 128], BF16,
                         base + 66048 + i * 8192) for i in range(2)]
        wnf = wn.ap().rearrange("p a b -> p (a b)")

        first = [("pe", self.v("pe")), ("va", self.v("va")), ("sa", self.v("sa"))]
        first += self.all_dma("st")
        ext_done = {}
        tp_done = {}
        stored = {}
        gr = 0
        for (wsrc, ssrc, dh, dl) in ((self.w1, self.s1, self.w1t_h, self.w1t_l),
                                     (self.w3, self.s3, self.w3t_h, self.w3t_l)):
            pkv = wsrc.rearrange("(h i) c -> h (i c)", i=DC)
            scv = ssrc.rearrange("(h g) -> h g", g=64)
            for rc in range(HC):
                sl = gr % 2
                w = [("va", ext_done.get(gr - 2, 0))] + (first if gr == 0 else [])
                self.load(pk[sl][:], pkv[rc * 128:(rc + 1) * 128, :], waits=w)
                ld1 = self.load(scb[sl][:], scv[rc * 128:(rc + 1) * 128, :], waits=w)

                def ext(e, sl=sl):
                    e.tensor_scalar(out=msb[:], in0=pk[sl][:], scalar1=4,
                                    scalar2=None, op0=ALU.arith_shift_right)
                    e.tensor_scalar(out=lsb[:], in0=pk[sl][:], scalar1=28, scalar2=28,
                                    op0=ALU.arith_shift_left,
                                    op1=ALU.arith_shift_right)
                    s_m = (scb[sl][:, 0:64:2].unsqueeze(-1)
                           .broadcast_to([128, 32, 64]))
                    s_l = (scb[sl][:, 1:64:2].unsqueeze(-1)
                           .broadcast_to([128, 32, 64]))
                    e.tensor_tensor(out=wn.ap()[:, :, 0:64],
                                    in0=msb[:].rearrange("p (i c) -> p i c", c=64),
                                    in1=s_m, op=ALU.mult)
                    return e.tensor_tensor(
                        out=wn.ap()[:, :, 64:128],
                        in0=lsb[:].rearrange("p (i c) -> p i c", c=64),
                        in1=s_l, op=ALU.mult)
                exw = [ld1, ("pe", tp_done.get(gr - 1, 0))] + (first if gr == 0 else [])
                ext_done[gr] = self.emit("vector", ext, waits=exw, inc="va")

                last_eh = last_el = 0
                for g in range(8):
                    srcs = [wnf[:, 128 * (4 * g + j):128 * (4 * g + j) + 128]
                            for j in range(4)]
                    osl = slice(g * 512, (g + 1) * 512)
                    stoks = stored.get(sl, [])
                    pd, eh, el = self.tp4(
                        g % 4, srcs, stg_h[sl][:, osl], stg_l[sl][:, osl],
                        tr_waits=[("va", ext_done[gr])] + (first if gr == 0 else []),
                        evh_waits=stoks + (first if gr == 0 else []),
                        evl_waits=stoks)
                    last_eh, last_el = eh, el
                tp_done[gr] = pd
                s1_ = self.store(dh[rc], stg_h[sl][:], waits=[("sa", last_eh)])
                s2_ = self.store(dl[rc], stg_l[sl][:], waits=[("va", last_el)])
                stored[sl] = [s1_, s2_]
                gr += 1
                yield
        while True:
            yield

    def ln_chunk(self, xsrc, xa, xb, dsth, dstl, pre_waits=(), sum_from_parts=None):
        last_ev = {}
        xln_done = {}
        for tt in range(8):
            sl = tt % 2
            lw = list(pre_waits) + [("va", xln_done.get(tt - 2, 0)),
                                    ("va", last_ev.get((tt - 2, 7), 0))]
            ldt = self.load(xa[sl][:], xsrc[tt], waits=lw)

            if sum_from_parts is None:
                def p1(e, sl=sl):
                    e.tensor_scalar(out=xb[sl][:], in0=xa[sl][:], scalar1=0.0,
                                    scalar2=None, op0=ALU.add, op1=ALU.add,
                                    accum_out=self.sx[sl][:])
                    e.drain()
                    e.tensor_scalar(out=self.mean[sl][:], in0=self.sx[sl][:],
                                    scalar1=1.0 / D, scalar2=None, op0=ALU.mult)
                    e.drain()
                    return e.tensor_scalar(out=xb[sl][:], in0=xa[sl][:],
                                           scalar1=self.mean[sl][:], scalar2=None,
                                           op0=ALU.subtract)
            else:
                psl = sum_from_parts(tt)

                def p1(e, sl=sl, psl=psl):
                    e.reduce_sum(out=self.sx[sl][:], in_=psl, axis=AX.X)
                    e.drain()
                    e.tensor_scalar(out=self.mean[sl][:], in0=self.sx[sl][:],
                                    scalar1=1.0 / D, scalar2=None, op0=ALU.mult)
                    e.drain()
                    return e.tensor_scalar(out=xb[sl][:], in0=xa[sl][:],
                                           scalar1=self.mean[sl][:], scalar2=None,
                                           op0=ALU.subtract)
            xc_done = self.emit("vector", p1, waits=[ldt], inc="va")

            def p2(e, sl=sl):
                e.activation(out=xa[sl][:], in_=xb[sl][:], func=AF.Square,
                             accum_out=self.sxc2[sl][:])
                e.drain()
                return e.activation(out=self.sdv[sl][:], in_=self.sxc2[sl][:],
                                    func=AF.Sqrt, scale=1.0 / D, bias=self.eps1[:])
            sq_done = self.emit("scalar", p2, waits=[("va", xc_done)], inc="sa")

            def p3(e, sl=sl):
                e.reciprocal(out=self.inv[sl][:], in_=self.sdv[sl][:])
                e.drain()
                return e.tensor_scalar(out=xa[sl][:], in0=xb[sl][:],
                                       scalar1=self.inv[sl][:], scalar2=None,
                                       op0=ALU.mult)
            xln_done[tt] = self.emit("vector", p3, waits=[("sa", sq_done)], inc="va")

            for g in range(8):
                srcs = [xa[sl][:, 128 * (4 * g + j):128 * (4 * g + j) + 128]
                        for j in range(4)]
                oh = dsth[:, 4 * g:4 * g + 4, tt * 128:(tt + 1) * 128]
                ol = dstl[:, 4 * g:4 * g + 4, tt * 128:(tt + 1) * 128]
                _, _, evl = self.tp4(g % 4, srcs, oh, ol,
                                     tr_waits=[("va", xln_done[tt])])
                last_ev[(tt, g)] = evl
        return self.v("va"), self.v("sa")

    def p_a(self):
        base = self.base
        xlnt_h = self.sb("xlnt_h", [128, DC, TOK], BF16, base)
        xlnt_l = self.sb("xlnt_l", [128, DC, TOK], BF16, base + 65536)
        xa = [self.sb(f"xaA{i}", [128, D], F32, base + 131072 + i * 16384)
              for i in range(2)]
        nat = xa
        xb = [self.sb(f"xbA{i}", [128, D], F32, base + 163840 + i * 16384)
              for i in range(2)]
        kst_h = [self.sb(f"kst_h{i}", [128, TOK], BF16, base + 163840 + i * 2048)
                 for i in range(2)]
        kst_l = [self.sb(f"kst_l{i}", [128, TOK], BF16, base + 167936 + i * 2048)
                 for i in range(2)]
        vst = [self.sb(f"vst{i}", [128, TOK], F32, base + 172032 + i * 4096)
               for i in range(2)]
        wt_h = self.sb("wt_hA", [128, D], BF16, base + 180224)
        wt_l = self.sb("wt_lA", [128, D], BF16, base + 188416)

        xo = self.x_own.rearrange("(t p) d -> t p d", p=128)
        xs = self.x_sib.rearrange("(t p) d -> t p d", p=128)
        wth3 = wt_h[:].rearrange("p (a b) -> p a b", b=128)
        wtl3 = wt_l[:].rearrange("p (a b) -> p a b", b=128)

        accpair = [(4, 5), (6, 7)]
        nat_free = {}
        st_done = {}
        wt_ev = {}
        for tc in range(2):
            xsrc = [xo[tt] if tc == 0 else xs[tt] for tt in range(8)]
            pre = (self.all_dma("st") + [("pe", self.v("pe")),
                                         ("va", self.v("va")),
                                         ("sa", self.v("sa"))])
            xln_ready, xln_ready_h = self.ln_chunk(xsrc, xa, xb,
                                                   xlnt_h[:], xlnt_l[:],
                                                   pre_waits=pre)
            if tc == 0:
                mats = [("q", self.wq, self.qT_h, self.qT_l, None),
                        ("k", self.wk, self.kT_h, self.kT_l, None),
                        ("v", self.wv, None, None, self.v_f)]
            else:
                mats = [("k", self.wk, self.kT_h, self.kT_l, None),
                        ("v", self.wv, None, None, self.v_f)]
            for mi, (mn, wsrc, dh, dl, df) in enumerate(mats):
                for oc in range(DC):
                    goc = self._a_goc = getattr(self, "_a_goc", 0)
                    self._a_goc += 1
                    sl = goc % 2
                    natw = [("pe", nat_free.get(goc - 2, 0)),
                            ("va", xln_ready), ("sa", xln_ready_h)]
                    ldt = self.load(nat[sl][:], wsrc[oc * 128:(oc + 1) * 128, :],
                                    waits=natw)
                    for g in range(8):
                        srcs = [nat[sl][:, 128 * (4 * g + j):128 * (4 * g + j) + 128]
                                for j in range(4)]
                        osl = slice(g * 512, (g + 1) * 512)
                        pd, eh, el = self.tp4(
                            g % 4, srcs, wt_h[:, osl], wt_l[:, osl],
                            tr_waits=[ldt,
                                      ("pe", wt_ev.get("mm", 0))],
                            evh_waits=[("pe", wt_ev.get("mm", 0))],
                            evl_waits=[("pe", wt_ev.get("mm", 0))])
                        if g == 7:
                            nat_free[goc] = pd
                            wt_ready, wt_ready_h = el, eh
                    pair = goc % 2
                    b0, b1 = accpair[pair]

                    def mms(e, b0=b0, b1=b1):
                        a0, a1 = self.banks[b0], self.banks[b1]
                        for dc in range(DC):
                            first = dc == 0
                            last = dc == DC - 1
                            lh = wth3[:, dc, :]
                            ll = wtl3[:, dc, :]
                            e.matmul(a0, lh, xlnt_h[:, dc, 0:512], start=first,
                                     stop=False)
                            e.matmul(a1, lh, xlnt_h[:, dc, 512:1024], start=first,
                                     stop=False)
                            e.matmul(a0, lh, xlnt_l[:, dc, 0:512], start=False,
                                     stop=False)
                            e.matmul(a1, lh, xlnt_l[:, dc, 512:1024], start=False,
                                     stop=False)
                            e.matmul(a0, ll, xlnt_h[:, dc, 0:512], start=False,
                                     stop=last)
                            inst = e.matmul(a1, ll, xlnt_h[:, dc, 512:1024],
                                            start=False, stop=last)
                        return inst
                    mw = ([("va", wt_ready), ("sa", wt_ready_h)]
                          + self.bank_waits([b0, b1]))
                    md = self.emit("tensor", mms, waits=mw, inc="pe")
                    wt_ev["mm"] = md
                    stw = st_done.get(sl, [])
                    cs = slice(tc * TOK, (tc + 1) * TOK)
                    rs = slice(oc * 128, (oc + 1) * 128)
                    if df is None:
                        evh = self.emit(
                            "scalar",
                            lambda e, sl=sl, b0=b0, b1=b1: (
                                e.activation(out=kst_h[sl][:, 0:512],
                                             in_=self.banks[b0], func=AF.Copy),
                                e.activation(out=kst_h[sl][:, 512:1024],
                                             in_=self.banks[b1], func=AF.Copy))[-1],
                            waits=[("pe", md)] + stw, inc="sa")
                        evl = self.emit(
                            "vector",
                            lambda e, sl=sl, b0=b0, b1=b1: (
                                e.tensor_tensor(out=kst_l[sl][:, 0:512],
                                                in0=self.banks[b0],
                                                in1=kst_h[sl][:, 0:512],
                                                op=ALU.subtract),
                                e.tensor_tensor(out=kst_l[sl][:, 512:1024],
                                                in0=self.banks[b1],
                                                in1=kst_h[sl][:, 512:1024],
                                                op=ALU.subtract))[-1],
                            waits=[("sa", evh)] + stw, inc="va")
                        self.set_reader([b0, b1], "va", evl)
                        s1_ = self.store(dh[rs, cs], kst_h[sl][:], waits=[("sa", evh)])
                        s2_ = self.store(dl[rs, cs], kst_l[sl][:], waits=[("va", evl)])
                        st_done[sl] = [s1_, s2_]
                    else:
                        evf = self.emit(
                            "scalar",
                            lambda e, sl=sl, b0=b0, b1=b1: (
                                e.activation(out=vst[sl][:, 0:512],
                                             in_=self.banks[b0], func=AF.Copy),
                                e.activation(out=vst[sl][:, 512:1024],
                                             in_=self.banks[b1], func=AF.Copy))[-1],
                            waits=[("pe", md)] + stw, inc="sa")
                        self.set_reader([b0, b1], "sa", evf)
                        st_done[sl] = [self.store(df[rs, cs], vst[sl][:],
                                                  waits=[("sa", evf)])]
        self.a_store_done = self.all_dma("st")

    def p_b(self):
        base = self.base
        kh_h = [self.sb(f"kh_h{i}", [128, S], BF16, base + i * 4096) for i in range(2)]
        kh_l = [self.sb(f"kh_l{i}", [128, S], BF16, base + 8192 + i * 4096)
                for i in range(2)]
        vh = [self.sb(f"vh{i}", [128, S], F32, base + 16384 + i * 8192)
              for i in range(2)]
        qh_h = [self.sb(f"qh_h{i}", [128, TOK], BF16, base + 32768 + i * 2048)
                for i in range(2)]
        qh_l = [self.sb(f"qh_l{i}", [128, TOK], BF16, base + 36864 + i * 2048)
                for i in range(2)]
        vt_h = self.sb("vt_h", [128, S], BF16, base + 40960)
        vt_l = self.sb("vt_l", [128, S], BF16, base + 45056)
        P = [self.sb(f"Pbuf{i}", [128, S], F32, base + 49152 + i * 8192)
             for i in range(2)]
        pt_h = self.sb("pt_h", [128, 16, 512], BF16, base + 65536)
        pt_l = self.sb("pt_l", [128, 16, 512], BF16, base + 81920)
        yth = self.sb("yth", [128, 512], BF16, base + 98304)
        ytl = self.sb("ytl", [128, 512], BF16, base + 99328)
        vth3 = vt_h[:].rearrange("p (a b) -> p a b", b=128)
        vtl3 = vt_l[:].rearrange("p (a b) -> p a b", b=128)

        head_done = {}
        ydone = {"st": []}
        self._pbuf_tr = {}
        p13 = self._prep13_gen()
        for h in range(NH):
            for _ in range(6):
                next(p13)
            sl = h % 2
            rs = slice(h * 128, (h + 1) * 128)
            lw = self.a_store_done + [("pe", head_done.get(h - 2, 0))]
            ldtoks = [self.load(kh_h[sl][:], self.kT_h[rs, :], waits=lw),
                      self.load(kh_l[sl][:], self.kT_l[rs, :], waits=lw),
                      self.load(vh[sl][:], self.v_f[rs, :], waits=lw),
                      self.load(qh_h[sl][:], self.qT_h[rs, :], waits=lw),
                      self.load(qh_l[sl][:], self.qT_l[rs, :], waits=lw)]

            vt_prev = [("va", head_done.get(("vtev", h - 1), 0))]
            for g in range(4):
                srcs = [vh[sl][:, 128 * (4 * g + j):128 * (4 * g + j) + 128]
                        for j in range(4)]
                osl = slice(g * 512, (g + 1) * 512)
                _, eh, el = self.tp4(g % 4, srcs, vt_h[:, osl], vt_l[:, osl],
                                     tr_waits=ldtoks + vt_prev,
                                     evh_waits=vt_prev, evl_waits=vt_prev)
            head_done[("vtev", h)] = el
            vt_ready, vt_ready_h = el, eh

            pend = {}
            for qq in range(8):
                half = qq // 4
                qt = qq % 4
                qs = qt % 2
                st = [0, 1, 2, 3] if qq % 2 == 0 else [4, 5, 6, 7]
                big = self.psA if qq % 2 == 0 else self.psB

                def msc(e, sl=sl, qq=qq, st=st):
                    lh = qh_h[sl][:, qq * 128:(qq + 1) * 128]
                    ll = qh_l[sl][:, qq * 128:(qq + 1) * 128]
                    for kc in range(4):
                        pb = self.banks[st[kc]]
                        rh = kh_h[sl][:, kc * 512:(kc + 1) * 512]
                        rl = kh_l[sl][:, kc * 512:(kc + 1) * 512]
                        e.matmul(pb, lh, rh, start=True, stop=False)
                        e.matmul(pb, lh, rl, start=False, stop=False)
                        inst = e.matmul(pb, ll, rh, start=False, stop=True)
                    return inst
                scw = ([("va", vt_ready), ("sa", vt_ready_h)] + ldtoks
                       + self.bank_waits(st))
                sc_done = self.emit("tensor", msc, waits=scw, inc="pe")

                def vmax(e, qs=qs, big=big):
                    e.reduce_max(out=self.mrow[qs][:], in_=big[:, 0:2048],
                                 axis=AX.X)
                    e.drain()
                    return e.tensor_scalar(out=self.negm[qs][:],
                                           in0=self.mrow[qs][:], scalar1=-ISCALE,
                                           scalar2=None, op0=ALU.mult)
                mx_done = self.emit("vector", vmax,
                                    waits=[("pe", sc_done),
                                           ("va", head_done.get(("pn", h, qq - 2),
                                                                0))],
                                    inc="va")

                def sexp(e, qs=qs, qt=qt, big=big):
                    return e.activation(out=P[qt % 2][:], in_=big[:, 0:2048],
                                        func=AF.Exp, scale=ISCALE,
                                        bias=self.negm[qs][:],
                                        accum_out=self.zrow[qs][:])
                exp_done = self.emit(
                    "scalar", sexp,
                    waits=[("va", mx_done),
                           ("pe", self._pbuf_tr.get(qt % 2, 0))], inc="sa")
                self.set_reader(st, "sa", exp_done)

                def pnorm(e, qs=qs, qt=qt):
                    e.reciprocal(out=self.rz[qs][:], in_=self.zrow[qs][:])
                    e.drain()
                    return e.tensor_scalar(out=P[qt % 2][:], in0=P[qt % 2][:],
                                           scalar1=self.rz[qs][:], scalar2=None,
                                           op0=ALU.mult)
                pn_done = self.emit("vector", pnorm, waits=[("sa", exp_done)],
                                    inc="va")
                head_done[("pn", h, qq)] = pn_done
                pend[qq] = (qt, pn_done, st)

                def flush_tp(k):
                    qtk, pnk, stk = pend.pop(k)
                    pt_prev = head_done.get(("pv", h, k // 4 - 1),
                                            head_done.get(("pv", h - 1, 1), 0))
                    for g in range(4):
                        srcs = [P[qtk % 2][:,
                                128 * (4 * g + j):128 * (4 * g + j) + 128]
                                for j in range(4)]
                        oh = pt_h[:, 4 * g:4 * g + 4, qtk * 128:(qtk + 1) * 128]
                        ol = pt_l[:, 4 * g:4 * g + 4, qtk * 128:(qtk + 1) * 128]
                        pd2, eh2, el2 = self.tp4(stk[g], srcs, oh, ol,
                                                 tr_waits=[("va", pnk),
                                                           ("pe", pt_prev)],
                                                 evh_waits=[("pe", pt_prev)],
                                                 evl_waits=[("pe", pt_prev)])
                    self._pbuf_tr[qtk % 2] = pd2
                    return el2, eh2

                if qq - 1 in pend:
                    pt_ready, pt_ready_h = flush_tp(qq - 1)
                if qt == 3:
                    pt_ready, pt_ready_h = flush_tp(qq)

                    def mpv(e):
                        for kc in range(16):
                            first = kc == 0
                            last = kc == 15
                            lh = vth3[:, kc, :]
                            ll = vtl3[:, kc, :]
                            rh = pt_h[:, kc, :]
                            rl = pt_l[:, kc, :]
                            e.matmul(self.banks[3], lh, rh, start=first, stop=False)
                            e.matmul(self.banks[3], lh, rl, start=False, stop=False)
                            inst = e.matmul(self.banks[3], ll, rh, start=False,
                                            stop=last)
                        return inst
                    pvw = ([("va", pt_ready), ("sa", pt_ready_h)]
                           + self.bank_waits([3]))
                    pv_done = self.emit("tensor", mpv, waits=pvw, inc="pe")
                    head_done[("pv", h, half)] = pv_done
                    head_done[h] = pv_done

                    evh = self.emit(
                        "scalar",
                        lambda e: e.activation(out=yth[:], in_=self.banks[3],
                                               func=AF.Copy),
                        waits=[("pe", pv_done)] + ydone["st"], inc="sa")
                    evl = self.emit(
                        "vector",
                        lambda e: e.tensor_tensor(out=ytl[:], in0=self.banks[3],
                                                  in1=yth[:], op=ALU.subtract),
                        waits=[("sa", evh)] + ydone["st"], inc="va")
                    self.set_reader([3], "va", evl)
                    dsth = (self.attnT_h[half * 4:half * 4 + 4, :, rs]
                            .transpose([1, 0, 2]))
                    dstl = (self.attnT_l[half * 4:half * 4 + 4, :, rs]
                            .transpose([1, 0, 2]))
                    s1_ = self.store(dsth,
                                     yth[:].rearrange("p (a b) -> p a b", b=128),
                                     waits=[("sa", evh)])
                    s2_ = self.store(dstl,
                                     ytl[:].rearrange("p (a b) -> p a b", b=128),
                                     waits=[("va", evl)])
                    ydone["st"] = [s1_, s2_]
        for _ in range(2 * HC):
            next(p13)
        p13.close()
        self.b_store_done = self.all_dma("st")

    def p_c(self):
        base = self.base
        wot_h = self.sb("wot_h", [128, DC, 512], BF16, base)
        wot_l = self.sb("wot_l", [128, DC, 512], BF16, base + 32768)
        nat = [self.sb(f"natC{i}", [128, D], F32, base + 65536 + i * 16384)
               for i in range(2)]
        ath = [self.sb(f"ath{i}", [128, D], BF16, base + 98304 + i * 8192)
               for i in range(2)]
        atl = [self.sb(f"atl{i}", [128, D], BF16, base + 114688 + i * 8192)
               for i in range(2)]
        xsl = [self.sb(f"xslC{i}", [128, 512], F32, base + 131072 + i * 2048)
               for i in range(2)]
        x1st = [self.sb(f"x1stC{i}", [128, 512], F32, base + 135168 + i * 2048)
                for i in range(2)]

        mm_done = {}
        ev_done = {}
        st_done = {}
        nat_free = {}
        for og in range(8):
            last_mm_prev = mm_done.get((og - 1, 7), 0)
            for j in range(4):
                sl = j % 2
                ocr = og * 4 + j
                ldt = self.load(nat[sl][:], self.wo[ocr * 128:(ocr + 1) * 128, :],
                                waits=[("pe", nat_free.get((og, j - 2),
                                                           last_mm_prev))]
                                + self.b_store_done)
                for g in range(8):
                    srcs = [nat[sl][:, 128 * (4 * g + j2):128 * (4 * g + j2) + 128]
                            for j2 in range(4)]
                    oh = wot_h[:, 4 * g:4 * g + 4, j * 128:(j + 1) * 128]
                    ol = wot_l[:, 4 * g:4 * g + 4, j * 128:(j + 1) * 128]
                    pd, eh, el = self.tp4(g % 4, srcs, oh, ol,
                                          tr_waits=[ldt,
                                                    ("pe", last_mm_prev)],
                                          evh_waits=[("pe", last_mm_prev)],
                                          evl_waits=[("pe", last_mm_prev)])
                    if g == 7:
                        nat_free[(og, j)] = pd
            wot_ready, wot_ready_h = self.v("va"), self.v("sa")

            for tt in range(8):
                sl = tt % 2
                lw = self.b_store_done + [("pe", mm_done.get((og, tt - 2), 0))]
                t_ah = self.load(ath[sl][:], self.attnT_h[tt], waits=lw)
                t_al = self.load(atl[sl][:], self.attnT_l[tt], waits=lw)
                ldx = self.load(xsl[sl][:],
                                self.x_own[tt * 128:(tt + 1) * 128,
                                           og * 512:(og + 1) * 512],
                                waits=[("va", ev_done.get((og, tt - 2), 0))])
                ath3 = ath[sl][:].rearrange("p (a b) -> p a b", b=128)
                atl3 = atl[sl][:].rearrange("p (a b) -> p a b", b=128)
                bk = 4 + tt % 2

                def mmc(e, ath3=ath3, atl3=atl3, bk=bk):
                    pb = self.banks[bk]
                    for ec in range(DC):
                        first = ec == 0
                        last = ec == DC - 1
                        lh = ath3[:, ec, :]
                        ll = atl3[:, ec, :]
                        rh = wot_h[:, ec, :]
                        rl = wot_l[:, ec, :]
                        e.matmul(pb, lh, rh, start=first, stop=False)
                        e.matmul(pb, lh, rl, start=False, stop=False)
                        inst = e.matmul(pb, ll, rh, start=False, stop=last)
                    return inst
                mw = ([("va", wot_ready), ("sa", wot_ready_h), t_ah, t_al]
                      + self.bank_waits([bk]))
                md = self.emit("tensor", mmc, waits=mw, inc="pe")
                mm_done[(og, tt)] = md

                def evc(e, sl=sl, bk=bk, tt=tt, og=og):
                    return e.scalar_tensor_tensor(
                        out=x1st[sl][:], in0=self.banks[bk], scalar=1.0,
                        in1=xsl[sl][:], op0=ALU.mult, op1=ALU.add,
                        accum_out=self.parts[:, tt * 8 + og:tt * 8 + og + 1])
                ed = self.emit("vector", evc,
                               waits=[("pe", md), ldx] + st_done.get(sl, []),
                               inc="va")
                ev_done[(og, tt)] = ed
                self.set_reader([bk], "va", ed)
                st_done[sl] = [self.store(
                    self.x1[tt * 128:(tt + 1) * 128, og * 512:(og + 1) * 512],
                    x1st[sl][:], waits=[("va", ed)])]
        self.c_store_done = self.all_dma("st")

    def p_c2(self):
        base = self.base
        self.xln2t_h = self.sb("xln2t_h", [128, DC, TOK], BF16, base)
        self.xln2t_l = self.sb("xln2t_l", [128, DC, TOK], BF16, base + 65536)
        xa = [self.sb(f"xaC2{i}", [128, D], F32, base + 131072 + i * 16384)
              for i in range(2)]
        xb = [self.sb(f"xbC2{i}", [128, D], F32, base + 163840 + i * 16384)
              for i in range(2)]
        x1v = self.x1.rearrange("(t p) d -> t p d", p=128)
        pre = self.c_store_done + [("pe", self.v("pe")),
                                   ("va", self.v("va")), ("sa", self.v("sa"))]
        self.xln2_ready, self.xln2_ready_h = self.ln_chunk(
            [x1v[tt] for tt in range(8)], xa, xb, self.xln2t_h[:], self.xln2t_l[:],
            pre_waits=pre,
            sum_from_parts=lambda tt: self.parts[:, tt * 8:tt * 8 + 8])

    def p_d(self):
        base = self.base + 131072
        w1b_h = [self.sb(f"w1b_h{i}", [128, D], BF16, base + i * 8192)
                 for i in range(2)]
        w1b_l = [self.sb(f"w1b_l{i}", [128, D], BF16, base + 16384 + i * 8192)
                 for i in range(2)]
        w3b_h = [self.sb(f"w3b_h{i}", [128, D], BF16, base + 32768 + i * 8192)
                 for i in range(2)]
        w3b_l = [self.sb(f"w3b_l{i}", [128, D], BF16, base + 49152 + i * 8192)
                 for i in range(2)]
        t1 = self.sb("t1D", [128, TOK], F32, base + 65536)
        gf = self.sb("gfD", [128, TOK], F32, base + 69632)
        gh = self.sb("ghD", [128, TOK], BF16, base + 73728)
        gl = self.sb("glD", [128, TOK], BF16, base + 75776)

        mm_done = {}
        st_prev = []
        for hc in range(HC):
            sl = hc % 2
            lw = self.b_store_done + [("pe", mm_done.get(hc - 2, 0)),
                                      ("va", self.xln2_ready),
                                      ("sa", self.xln2_ready_h)]
            dtoks = [self.load(w1b_h[sl][:], self.w1t_h[hc], waits=lw),
                     self.load(w1b_l[sl][:], self.w1t_l[hc], waits=lw),
                     self.load(w3b_h[sl][:], self.w3t_h[hc], waits=lw),
                     self.load(w3b_l[sl][:], self.w3t_l[hc], waits=lw)]
            seta = hc % 2
            acc = [0, 1, 2, 3] if seta == 0 else [4, 5, 6, 7]

            def mmd(e, sl=sl, acc=acc):
                a0, a1 = self.banks[acc[0]], self.banks[acc[1]]
                b0, b1 = self.banks[acc[2]], self.banks[acc[3]]
                w1h3 = w1b_h[sl][:].rearrange("p (a b) -> p a b", b=128)
                w1l3 = w1b_l[sl][:].rearrange("p (a b) -> p a b", b=128)
                w3h3 = w3b_h[sl][:].rearrange("p (a b) -> p a b", b=128)
                w3l3 = w3b_l[sl][:].rearrange("p (a b) -> p a b", b=128)
                for dc in range(DC):
                    first = dc == 0
                    last = dc == DC - 1
                    xh0 = self.xln2t_h[:, dc, 0:512]
                    xh1 = self.xln2t_h[:, dc, 512:1024]
                    xl0 = self.xln2t_l[:, dc, 0:512]
                    xl1 = self.xln2t_l[:, dc, 512:1024]
                    for (wh3, wl3, c0, c1) in ((w1h3, w1l3, a0, a1),
                                               (w3h3, w3l3, b0, b1)):
                        lh = wh3[:, dc, :]
                        ll = wl3[:, dc, :]
                        e.matmul(c0, lh, xh0, start=first, stop=False)
                        e.matmul(c1, lh, xh1, start=first, stop=False)
                        e.matmul(c0, lh, xl0, start=False, stop=False)
                        e.matmul(c1, lh, xl1, start=False, stop=False)
                        e.matmul(c0, ll, xh0, start=False, stop=last)
                        inst = e.matmul(c1, ll, xh1, start=False, stop=last)
                return inst
            mw = dtoks + self.bank_waits(acc)
            md = self.emit("tensor", mmd, waits=mw, inc="pe")
            mm_done[hc] = md

            def silu(e, acc=acc):
                big = self.psA if acc[0] == 0 else self.psB
                return e.activation(out=t1[:], in_=big[:, 0:1024], func=AF.Silu)
            sd = self.emit("scalar", silu, waits=[("pe", md)], inc="sa")
            self.set_reader(acc[0:2], "sa", sd)

            def gmul(e, acc=acc):
                big = self.psA if acc[0] == 0 else self.psB
                return e.tensor_tensor(out=gf[:], in0=big[:, 1024:2048],
                                       in1=t1[:], op=ALU.mult)
            gd = self.emit("vector", gmul, waits=[("sa", sd)], inc="va")
            self.set_reader(acc[2:4], "va", gd)

            hd = self.emit("scalar",
                           lambda e: e.activation(out=gh[:], in_=gf[:], func=AF.Copy),
                           waits=[("va", gd)] + st_prev, inc="sa")
            ldn = self.emit("vector",
                            lambda e: e.tensor_tensor(out=gl[:], in0=gf[:], in1=gh[:],
                                                      op=ALU.subtract),
                            waits=[("sa", hd)] + st_prev, inc="va")
            s1_ = self.store(self.g_h[hc], gh[:], waits=[("sa", hd)])
            s2_ = self.store(self.g_l[hc], gl[:], waits=[("va", ldn)])
            st_prev = [s1_, s2_]
        self.d_store_done = self.all_dma("st")

    def p_e(self):
        base = self.base
        w2b_h = [self.sb(f"w2b_h{i}", [128, 512], BF16, base + i * 1024)
                 for i in range(2)]
        w2b_l = [self.sb(f"w2b_l{i}", [128, 512], BF16, base + 2048 + i * 1024)
                 for i in range(2)]
        gbh = [self.sb(f"gbh{i}", [128, TOK], BF16, base + 4096 + i * 2048)
               for i in range(2)]
        gbl = [self.sb(f"gbl{i}", [128, TOK], BF16, base + 8192 + i * 2048)
               for i in range(2)]
        x1b = [self.sb(f"x1bE{i}", [128, 512], F32, base + 12288 + i * 2048)
               for i in range(2)]
        outst = [self.sb(f"outstE{i}", [128, 512], F32, base + 16384 + i * 2048)
                 for i in range(2)]

        ev_done = {}
        mm_prog = {}
        so_done = {}
        for og in range(8):
            bank_free = self.bank_waits(list(range(8)))
            for hc in range(HC):
                sl = hc % 2
                lw = self.d_store_done + [("pe", mm_prog.get((og, hc - 2), 0))]
                if hc < 2:
                    lw = lw + [("va", ev_done.get((og - 1, "all"), 0))]
                etoks = [self.load(w2b_h[sl][:],
                                   self.w2t_h[hc][:, og * 512:(og + 1) * 512],
                                   waits=lw),
                         self.load(w2b_l[sl][:],
                                   self.w2t_l[hc][:, og * 512:(og + 1) * 512],
                                   waits=lw),
                         self.load(gbh[sl][:], self.g_h[hc], waits=lw),
                         self.load(gbl[sl][:], self.g_l[hc], waits=lw)]

                def mme(e, sl=sl, hc=hc):
                    first = hc == 0
                    last = hc == HC - 1
                    for tt in range(8):
                        lh = gbh[sl][:, tt * 128:(tt + 1) * 128]
                        ll = gbl[sl][:, tt * 128:(tt + 1) * 128]
                        pb = self.banks[tt]
                        e.matmul(pb, lh, w2b_h[sl][:], start=first, stop=False)
                        e.matmul(pb, lh, w2b_l[sl][:], start=False, stop=False)
                        inst = e.matmul(pb, ll, w2b_h[sl][:], start=False, stop=last)
                    return inst
                mw = etoks + (bank_free if hc == 0 else [])
                md = self.emit("tensor", mme, waits=mw, inc="pe")
                mm_prog[(og, hc)] = md
            og_done = mm_prog[(og, HC - 1)]
            for tt in range(8):
                sl = tt % 2
                ldx = self.load(x1b[sl][:],
                                self.x1[tt * 128:(tt + 1) * 128,
                                        og * 512:(og + 1) * 512],
                                waits=self.c_store_done
                                + [("va", ev_done.get((og, tt - 2), 0))])

                def eve(e, sl=sl, tt=tt):
                    return e.scalar_tensor_tensor(
                        out=outst[sl][:], in0=self.banks[tt], scalar=1.0,
                        in1=x1b[sl][:], op0=ALU.mult, op1=ALU.add)
                ed = self.emit("vector", eve,
                               waits=[("pe", og_done), ldx] + so_done.get(sl, []),
                               inc="va")
                ev_done[(og, tt)] = ed
                self.set_reader([tt], "va", ed)
                so_done[sl] = [self.store(
                    self.out[tt * 128:(tt + 1) * 128, og * 512:(og + 1) * 512],
                    outst[sl][:], waits=[("va", ed)], pool="so")]
        self.emit("scalar", lambda e: e.nop(), waits=self.all_dma("so"))

    def build(self):
        self.p_init()
        if self.upto >= 1:
            self.p_prep()
        else:
            self.prep_store_done = []
        if self.upto >= 2:
            self.p_a()
        if self.upto >= 3:
            self.p_b()
        if self.upto >= 4:
            self.p_c()
        if self.upto >= 5:
            self.p_c2()
        if self.upto >= 6:
            self.p_d()
        if self.upto >= 7:
            self.p_e()
        else:
            self.emit("scalar", lambda e: e.nop(),
                      waits=self.all_dma("st") + self.all_dma("ld"))
        return self.finalize()


_NC_CACHE = {}


def get_nc(dbg=False, upto=9):
    key = (dbg, upto)
    if key not in _NC_CACHE:
        _NC_CACHE[key] = Builder(dbg=dbg, upto=upto).build()
    return _NC_CACHE[key]


def make_in_maps(x, wq, wk, wv, wo, w1, s1, w2, s2, w3, s3):
    in_maps = []
    for c in range(8):
        b, hf = c // 2, c % 2
        in_maps.append({
            "x_own": np.ascontiguousarray(x[b, hf * TOK:(hf + 1) * TOK]),
            "x_sib": np.ascontiguousarray(x[b, (1 - hf) * TOK:(2 - hf) * TOK]),
            "wq": wq, "wk": wk, "wv": wv, "wo": wo,
            "w1": w1, "w2": w2, "w3": w3, "s1": s1, "s2": s2, "s3": s3,
        })
    return in_maps


def kernel(L0_attn_x, L0_attn_w_q, L0_attn_w_k, L0_attn_w_v, L0_attn_w_o,
           L0_ffn_w1, L0_ffn_s1, L0_ffn_w2, L0_ffn_s2, L0_ffn_w3, L0_ffn_s3):
    asc = np.ascontiguousarray
    x = asc(np.asarray(L0_attn_x, dtype=np.float32))
    args = (asc(np.asarray(L0_attn_w_q, dtype=np.float32)),
            asc(np.asarray(L0_attn_w_k, dtype=np.float32)),
            asc(np.asarray(L0_attn_w_v, dtype=np.float32)),
            asc(np.asarray(L0_attn_w_o, dtype=np.float32)),
            asc(np.asarray(L0_ffn_w1, dtype=np.int32)),
            asc(np.asarray(L0_ffn_s1, dtype=np.float32)),
            asc(np.asarray(L0_ffn_w2, dtype=np.int32)),
            asc(np.asarray(L0_ffn_s2, dtype=np.float32)),
            asc(np.asarray(L0_ffn_w3, dtype=np.int32)),
            asc(np.asarray(L0_ffn_s3, dtype=np.float32)))
    nc = get_nc()
    in_maps = make_in_maps(x, *args)
    res = run_bass_kernel_spmd(nc, in_maps, core_ids=list(range(8)))
    y = np.empty((4, S, D), dtype=np.float32)
    for c in range(8):
        b, hf = c // 2, c % 2
        y[b, hf * TOK:(hf + 1) * TOK] = res.results[c]["out"]
    return y
